# version 01
# baseline (speedup 1.0000x reference)
import contextlib
import numpy as np
import ml_dtypes
import concourse.bass as bass
import concourse.mybir as mybir
from concourse.bass_utils import run_bass_kernel_spmd

F32 = mybir.dt.float32
BF16 = mybir.dt.bfloat16
OP = mybir.AluOpType
AF = mybir.ActivationFunctionType
AX = mybir.AxisListType.X

MAGIC = 1.5 * 2.0 ** 23
D = 1024
I = 2816
NKD = D // 128
NKI = I // 128
TB = 512
NG = TB // 128
SEQ = 4096
NSEQ = 2
NBLK = NSEQ * SEQ // TB
BPS = SEQ // TB
EPS = 1e-5
RD = 1.0 / D


def _tm(ap2d):
    return ap2d.rearrange("(g p) d -> p g d", p=128)


def _g3(sbuf_ap):
    return sbuf_ap.rearrange("p (g d) -> p g d", g=NG)


def _fm_vec(ap1d):
    return ap1d.rearrange("(k p) -> p k", p=128)


def _bcast_col(eng, tile, width, sem, base):
    p = 1
    n = 0
    while p < 128:
        eng.dma_start(out=tile[p:min(2 * p, 128), 0:width],
                      in_=tile[0:min(p, 128 - p), 0:width]).then_inc(sem, 16)
        n += 1
        eng.wait_ge(sem, base + 16 * n)
        p *= 2
    return n


def build_phase_a():
    nc = bass.Bass("TRN2", target_bir_lowering=False, detect_race_conditions=False)
    hb = {}
    hb["x"] = nc.dram_tensor("x", [NSEQ * SEQ, D], F32, kind="ExternalInput").ap()
    hb["W_in"] = nc.dram_tensor("W_in", [D, D], F32, kind="ExternalInput").ap()
    hb["C"] = nc.dram_tensor("C", [D, D], F32, kind="ExternalInput").ap()
    for nm in ["log_A", "B_vec", "b_in", "Dv"]:
        hb[nm] = nc.dram_tensor(nm, [D], F32, kind="ExternalInput").ap()
    hb["eye_bf"] = nc.dram_tensor("eye_bf", [128, 128], BF16, kind="ExternalInput").ap()
    hb["eye_f"] = nc.dram_tensor("eye_f", [128, 128], F32, kind="ExternalInput").ap()
    hb["xmid"] = nc.dram_tensor("xmid", [NSEQ * SEQ, D], F32, kind="ExternalOutput").ap()
    hb["m2_all"] = nc.dram_tensor("m2_all", [128, NBLK * NG], F32, kind="ExternalOutput").ap()
    hb["r2_all"] = nc.dram_tensor("r2_all", [128, NBLK * NG], F32, kind="ExternalOutput").ap()
    hb["max1"] = nc.dram_tensor("max1", [1, 1], F32, kind="ExternalOutput").ap()

    es = contextlib.ExitStack()
    E = es.enter_context
    sb = {}

    def S_(name, shape, dtype):
        sb[name] = E(nc.sbuf_tensor(name, shape, dtype))

    S_("xa", [128, 2 * NG * D], F32)
    S_("sq_scr", [128, NG * D], F32)
    S_("sq_scr2", [128, NG * D], F32)
    S_("xn_tm", [128, NG * D], BF16)
    S_("xn_f", [128, NKD * TB], BF16)
    S_("b_scan", [128, NKD * TB], BF16)
    S_("h", [128, 2 * NKD * TB], BF16)
    S_("xmid_s", [128, NG * D], F32)
    S_("d_scr", [128, D], F32)
    S_("W_bf", [128, NKD * D], BF16)
    S_("C_bf", [128, NKD * D], BF16)
    S_("wstage", [128, NKD * D], F32)
    S_("eye_bf_s", [128, 128], BF16)
    S_("eye_f_s", [128, 128], F32)
    S_("D_bc", [128, D], BF16)
    S_("logA_f", [128, NKD], F32)
    S_("a_f", [128, NKD], F32)
    S_("B_f", [128, NKD], F32)
    S_("bin_f", [128, NKD], F32)
    S_("Bb_f", [128, NKD], F32)
    S_("Dv_row", [1, D], F32)
    S_("st1", [128, 4 * NG], F32)
    S_("st2", [128, 4 * NG], F32)
    S_("m2_all_s", [128, NBLK * NG], F32)
    S_("r2_all_s", [128, NBLK * NG], F32)
    S_("macc", [128, 1], F32)
    S_("mrow", [1, 128], F32)
    S_("max1_s", [1, 1], F32)

    ps = {}
    ps["pT0"] = E(nc.psum_tensor("pT0", [128, 1024], BF16))
    ps["pT1"] = E(nc.psum_tensor("pT1", [128, 1024], BF16))
    ps["pu"] = E(nc.psum_tensor("pu", [128, 2 * TB], F32))
    ps["py"] = E(nc.psum_tensor("py", [128, D], F32))
    ps["pm"] = E(nc.psum_tensor("pm", [1, 128], F32))

    sems = {}
    for s in ["s_x", "s_sq1", "s_v1", "s_rstd", "s_xn", "s_tT", "s_tev", "s_bfree",
              "s_umm", "s_scan", "s_ymm", "s_yfree", "s_xm", "s_sq2", "s_v2",
              "s_rstd2", "s_xmdone", "s_st", "s_wv", "s_ww", "s_dbb", "s_wcast", "s_dbc", "s_fin", "s_vs"]:
        sems[s] = E(nc.semaphore(s))
    lo = min(h.num for h in sems.values())
    hi = max(h.num for h in sems.values())
    g = sems
    pT = [ps["pT0"], ps["pT1"]]

    with nc.Block() as blk:
        @blk.sync
        def _(sync):
            for name, src in [("logA_f", _fm_vec(hb["log_A"])), ("B_f", _fm_vec(hb["B_vec"])),
                              ("bin_f", _fm_vec(hb["b_in"])), ("Dv_row", hb["Dv"].unsqueeze(0)),
                              ("eye_bf_s", hb["eye_bf"]), ("eye_f_s", hb["eye_f"])]:
                sync.dma_start(out=sb[name][:], in_=src,
                               allow_slow_non_contiguous=True).then_inc(g["s_wv"], 16)
            sync.dma_start(out=sb["wstage"][:].rearrange("p (k d) -> p k d", k=NKD),
                           in_=hb["W_in"].rearrange("(k p) d -> p k d", p=128)).then_inc(g["s_ww"], 16)
            sync.wait_ge(g["s_wcast"], 1)
            sync.dma_start(out=sb["wstage"][:].rearrange("p (k d) -> p k d", k=NKD),
                           in_=hb["C"].rearrange("(k p) d -> p k d", p=128)).then_inc(g["s_ww"], 16)
            sync.wait_ge(g["s_dbc"], 1)
            _bcast_col(sync, sb["D_bc"], D, g["s_dbb"], 0)
            for j in range(NBLK):
                if j >= 2:
                    sync.wait_ge(g["s_xm"], 4 * (j - 2) + 4)
                if j >= 1:
                    sync.wait_ge(g["s_x"], 16 * j)
                sync.dma_start(out=_g3(sb["xa"][:, (j % 2) * NG * D:(j % 2 + 1) * NG * D]),
                               in_=_tm(hb["x"][j * TB:(j + 1) * TB, :])).then_inc(g["s_x"], 16)
                if j >= 1:
                    sync.wait_ge(g["s_xmdone"], j)
                    if j >= 2:
                        sync.wait_ge(g["s_st"], 16 * (j - 1))
                    sync.dma_start(out=_tm(hb["xmid"][(j - 1) * TB:j * TB, :]),
                                   in_=_g3(sb["xmid_s"][:])).then_inc(g["s_st"], 16)
            sync.wait_ge(g["s_xmdone"], NBLK)
            sync.wait_ge(g["s_st"], 16 * (NBLK - 1))
            sync.dma_start(out=_tm(hb["xmid"][(NBLK - 1) * TB:NBLK * TB, :]),
                           in_=_g3(sb["xmid_s"][:])).then_inc(g["s_st"], 16)
            sync.wait_ge(g["s_fin"], 3)
            sync.dma_start(out=hb["m2_all"], in_=sb["m2_all_s"][:]).then_inc(g["s_st"], 16)
            sync.dma_start(out=hb["r2_all"], in_=sb["r2_all_s"][:]).then_inc(g["s_st"], 16)
            sync.dma_start(out=hb["max1"], in_=sb["max1_s"][:]).then_inc(g["s_st"], 16)
            sync.wait_ge(g["s_st"], 16 * (NBLK + 3))

        @blk.scalar
        def _(scalar):
            scalar.wait_ge(g["s_ww"], 16)
            scalar.copy(sb["W_bf"][:], sb["wstage"][:]).then_inc(g["s_wcast"], 1)
            scalar.wait_ge(g["s_ww"], 32)
            scalar.copy(sb["C_bf"][:], sb["wstage"][:]).then_inc(g["s_wcast"], 1)
            scalar.wait_ge(g["s_wv"], 16 * 6)
            scalar.activation(sb["a_f"][:], sb["logA_f"][:], AF.Exp).then_inc(g["s_wcast"], 1)
            for j in range(NBLK):
                o = (j % 2) * NG * D
                scalar.wait_ge(g["s_x"], 16 * (j + 1))
                for gg in range(NG):
                    ins = scalar.activation(sb["sq_scr2"][:, gg * D:(gg + 1) * D],
                                            sb["xa"][:, o + gg * D:o + (gg + 1) * D],
                                            AF.Square, accum_out=sb["st1"][:, NG + gg:NG + gg + 1])
                    if gg == NG - 1:
                        ins.then_inc(g["s_sq1"], 1)
                scalar.wait_ge(g["s_v1"], j + 1)
                scalar.activation(sb["st1"][:, 3 * NG:4 * NG], sb["st1"][:, 2 * NG:3 * NG],
                                  AF.Sqrt).then_inc(g["s_rstd"], 1)
                for k in range(NKD):
                    scalar.wait_ge(g["s_tT"], NKD * j + k + 1)
                    scalar.copy(sb["xn_f"][:, k * TB:(k + 1) * TB],
                                pT[k % 2][:, 0:TB]).then_inc(g["s_tev"], 1)
                for gg in range(NG):
                    scalar.wait_ge(g["s_xm"], 4 * j + gg + 1)
                    ins = scalar.activation(sb["sq_scr2"][:, gg * D:(gg + 1) * D],
                                            sb["xmid_s"][:, gg * D:(gg + 1) * D],
                                            AF.Square, accum_out=sb["st2"][:, NG + gg:NG + gg + 1])
                    if gg == NG - 1:
                        ins.then_inc(g["s_sq2"], 1)
                scalar.wait_ge(g["s_v2"], j + 1)
                scalar.activation(sb["st2"][:, 3 * NG:4 * NG], sb["st2"][:, 2 * NG:3 * NG],
                                  AF.Sqrt).then_inc(g["s_rstd2"], 1)

        @blk.tensor
        def _(tensor):
            tensor.wait_ge(g["s_wv"], 16 * 6)
            for j in range(NBLK):
                tensor.wait_ge(g["s_xn"], j + 1)
                for k in range(NKD):
                    if NKD * j + k >= 2:
                        tensor.wait_ge(g["s_tev"], NKD * j + k - 1)
                    for gg in range(NG):
                        ins = tensor.transpose(pT[k % 2][:, gg * 128:(gg + 1) * 128],
                                               sb["xn_tm"][:, gg * D + k * 128:gg * D + (k + 1) * 128],
                                               sb["eye_bf_s"][:])
                        if gg == NG - 1:
                            ins.then_inc(g["s_tT"], 1)
                tensor.wait_ge(g["s_tev"], NKD * j + NKD)
                for m in range(NKD):
                    if NKD * j + m >= 2:
                        tensor.wait_ge(g["s_bfree"], NKD * j + m - 1)
                    for k in range(NKD):
                        ins = tensor.matmul(ps["pu"][:, (m % 2) * TB:(m % 2 + 1) * TB],
                                            sb["W_bf"][:, k * D + m * 128:k * D + (m + 1) * 128],
                                            sb["xn_f"][:, k * TB:(k + 1) * TB],
                                            start=(k == 0), stop=(k == NKD - 1))
                        if k == NKD - 1:
                            ins.then_inc(g["s_umm"], 1)
                tensor.wait_ge(g["s_scan"], j + 1)
                for gg in range(NG):
                    if 4 * j + gg >= 1:
                        tensor.wait_ge(g["s_yfree"], 4 * j + gg)
                    for k in range(NKD):
                        for hf in range(2):
                            ins = tensor.matmul(ps["py"][:, hf * TB:(hf + 1) * TB],
                                                sb["h"][:, (j % 2) * NKD * TB + k * TB + gg * 128:
                                                        (j % 2) * NKD * TB + k * TB + (gg + 1) * 128],
                                                sb["C_bf"][:, k * D + hf * TB:k * D + (hf + 1) * TB],
                                                start=(k == 0), stop=(k == NKD - 1))
                            if k == NKD - 1 and hf == 1:
                                ins.then_inc(g["s_ymm"], 1)
            tensor.wait_ge(g["s_fin"], 1)
            tensor.transpose(ps["pm"][0:1, :], sb["macc"][:],
                             sb["eye_f_s"][:]).then_inc(g["s_fin"], 1)

        @blk.vector
        def _(vector):
            vsn = [0]

            def vq(ins):
                vsn[0] += 1
                ins.then_inc(g["s_vs"], 1)
                vector.wait_ge(g["s_vs"], vsn[0])
            vector.wait_ge(g["s_wcast"], 3)
            vq(vector.tensor_scalar(sb["a_f"][:], sb["a_f"][:], -1.0, None, OP.mult))
            vq(vector.tensor_tensor(sb["Bb_f"][:], sb["B_f"][:], sb["bin_f"][:], OP.mult))
            vq(vector.memset(sb["macc"][:], 0.0))
            vector.tensor_copy(sb["D_bc"][0:1, :], sb["Dv_row"][0:1, :]).then_inc(g["s_dbc"], 1)
            vector.wait_ge(g["s_dbb"], 16 * 7)
            for j in range(NBLK):
                o = (j % 2) * NG * D
                vector.wait_ge(g["s_x"], 16 * (j + 1))
                for gg in range(NG):
                    ins = vector.tensor_scalar(sb["sq_scr"][:, gg * D:(gg + 1) * D],
                                               sb["xa"][:, o + gg * D:o + (gg + 1) * D],
                                               1.0, 0.0, OP.mult, OP.add,
                                               accum_out=sb["st1"][:, gg:gg + 1])
                    ins.then_inc(g["s_vs"], 1)
                vsn[0] += NG
                vector.wait_ge(g["s_vs"], vsn[0])
                vector.wait_ge(g["s_sq1"], j + 1)
                vq(vector.tensor_scalar(sb["st1"][:, 0:NG], sb["st1"][:, 0:NG], RD, None, OP.mult))
                vq(vector.tensor_scalar(sb["st1"][:, NG:2 * NG], sb["st1"][:, NG:2 * NG], RD, None, OP.mult))
                vq(vector.tensor_tensor(sb["st1"][:, 2 * NG:3 * NG], sb["st1"][:, 0:NG], sb["st1"][:, 0:NG], OP.mult))
                vq(vector.tensor_tensor(sb["st1"][:, 2 * NG:3 * NG], sb["st1"][:, NG:2 * NG],
                                        sb["st1"][:, 2 * NG:3 * NG], OP.subtract))
                vq(vector.tensor_scalar(sb["st1"][:, 2 * NG:3 * NG], sb["st1"][:, 2 * NG:3 * NG], EPS, None, OP.add))
                vector.reciprocal(sb["st1"][:, 2 * NG:3 * NG],
                                  sb["st1"][:, 2 * NG:3 * NG]).then_inc(g["s_v1"], 1)
                vector.wait_ge(g["s_rstd"], j + 1)
                for gg in range(NG):
                    ins = vector.tensor_scalar(sb["xn_tm"][:, gg * D:(gg + 1) * D],
                                               sb["xa"][:, o + gg * D:o + (gg + 1) * D],
                                               sb["st1"][:, gg:gg + 1],
                                               sb["st1"][:, 3 * NG + gg:3 * NG + gg + 1],
                                               OP.subtract, OP.mult)
                    if gg == NG - 1:
                        ins.then_inc(g["s_xn"], 1)
                for m in range(NKD):
                    vector.wait_ge(g["s_umm"], NKD * j + m + 1)
                    vq(vector.tensor_scalar(sb["b_scan"][:, m * TB:(m + 1) * TB],
                                            ps["pu"][:, (m % 2) * TB:(m % 2 + 1) * TB],
                                            sb["B_f"][:, m:m + 1], sb["Bb_f"][:, m:m + 1],
                                            OP.mult, OP.add))
                    vector.nop().then_inc(g["s_bfree"], 1)
                for k in range(NKD):
                    first = (j % BPS == 0)
                    init = 0.0 if first else sb["h"][:, ((j - 1) % 2) * NKD * TB + (k + 1) * TB - 1:
                                                     ((j - 1) % 2) * NKD * TB + (k + 1) * TB]
                    vq(vector.tensor_tensor_scan(
                        sb["h"][:, (j % 2) * NKD * TB + k * TB:(j % 2) * NKD * TB + (k + 1) * TB],
                        sb["a_f"][:, k:k + 1].broadcast_to((128, TB)),
                        sb["b_scan"][:, k * TB:(k + 1) * TB],
                        init, OP.mult, OP.add))
                    vector.drain()
                vector.nop().then_inc(g["s_scan"], 1)
                for gg in range(NG):
                    vector.wait_ge(g["s_ymm"], 4 * j + gg + 1)
                    vq(vector.tensor_tensor(sb["d_scr"][:], sb["xn_tm"][:, gg * D:(gg + 1) * D],
                                            sb["D_bc"][:], OP.mult))
                    vq(vector.tensor_tensor(sb["xmid_s"][:, gg * D:(gg + 1) * D],
                                            sb["xa"][:, o + gg * D:o + (gg + 1) * D], ps["py"][:], OP.add))
                    vector.tensor_tensor(sb["xmid_s"][:, gg * D:(gg + 1) * D],
                                         sb["xmid_s"][:, gg * D:(gg + 1) * D],
                                         sb["d_scr"][:], OP.add).then_inc(g["s_yfree"], 1)
                    vector.wait_ge(g["s_yfree"], 4 * j + gg + 1)
                    vector.tensor_scalar(sb["sq_scr"][:, gg * D:(gg + 1) * D],
                                         sb["xmid_s"][:, gg * D:(gg + 1) * D],
                                         1.0, 0.0, OP.mult, OP.add,
                                         accum_out=sb["st2"][:, gg:gg + 1]).then_inc(g["s_vs"], 1)
                    vsn[0] += 1
                    vector.nop().then_inc(g["s_xm"], 1)
                vector.wait_ge(g["s_vs"], vsn[0])
                vector.wait_ge(g["s_sq2"], j + 1)
                vq(vector.tensor_scalar(sb["st2"][:, 0:NG], sb["st2"][:, 0:NG], RD, None, OP.mult))
                vq(vector.tensor_scalar(sb["st2"][:, NG:2 * NG], sb["st2"][:, NG:2 * NG], RD, None, OP.mult))
                vq(vector.tensor_tensor(sb["st2"][:, 2 * NG:3 * NG], sb["st2"][:, 0:NG], sb["st2"][:, 0:NG], OP.mult))
                vq(vector.tensor_tensor(sb["st2"][:, 2 * NG:3 * NG], sb["st2"][:, NG:2 * NG],
                                        sb["st2"][:, 2 * NG:3 * NG], OP.subtract))
                vq(vector.tensor_scalar(sb["st2"][:, 2 * NG:3 * NG], sb["st2"][:, 2 * NG:3 * NG], EPS, None, OP.add))
                vector.reciprocal(sb["st2"][:, 2 * NG:3 * NG],
                                  sb["st2"][:, 2 * NG:3 * NG]).then_inc(g["s_v2"], 1)
                vector.wait_ge(g["s_rstd2"], j + 1)
                vector.tensor_copy(sb["m2_all_s"][:, j * NG:(j + 1) * NG], sb["st2"][:, 0:NG])
                vector.tensor_copy(sb["r2_all_s"][:, j * NG:(j + 1) * NG], sb["st2"][:, 3 * NG:4 * NG])
                for gg in range(NG):
                    vq(vector.tensor_scalar(sb["d_scr"][:], sb["xmid_s"][:, gg * D:(gg + 1) * D],
                                            sb["st2"][:, gg:gg + 1], None, OP.subtract))
                    vq(vector.tensor_reduce(sb["st2"][:, 2 * NG + gg:2 * NG + gg + 1], sb["d_scr"][:],
                                            AX, OP.max, apply_absolute_value=True))
                vq(vector.tensor_tensor(sb["st2"][:, 2 * NG:3 * NG], sb["st2"][:, 2 * NG:3 * NG],
                                        sb["st2"][:, 3 * NG:4 * NG], OP.mult))
                vq(vector.tensor_reduce(sb["d_scr"][:, 0:1], sb["st2"][:, 2 * NG:3 * NG], AX, OP.max))
                vector.tensor_tensor(sb["macc"][:], sb["macc"][:], sb["d_scr"][:, 0:1],
                                     OP.max).then_inc(g["s_xmdone"], 1)
                vector.wait_ge(g["s_xmdone"], j + 1)
                vector.wait_ge(g["s_st"], 16 * (j + 1))
            vector.nop().then_inc(g["s_fin"], 1)
            vector.wait_ge(g["s_fin"], 2)
            vq(vector.tensor_copy(sb["mrow"][:], ps["pm"][0:1, :]))
            vector.tensor_reduce(sb["max1_s"][:], sb["mrow"][:], AX, OP.max).then_inc(g["s_fin"], 1)

    nc.gpsimd.dma_reset(range(lo, hi + 1))
    nc.gpsimd.sem_clear(range(lo, hi + 1))
    es.close()
    return nc


def _stage_phase_a(inputs, core):
    x = inputs["L0_attn_x"]
    return {
        "x": np.ascontiguousarray(x[2 * core:2 * core + 2].reshape(NSEQ * SEQ, D)),
        "W_in": np.ascontiguousarray(inputs["L0_attn_W_in"]),
        "C": np.ascontiguousarray(inputs["L0_attn_C"]),
        "log_A": np.ascontiguousarray(inputs["L0_attn_log_A"]),
        "B_vec": np.ascontiguousarray(inputs["L0_attn_B_vec"]),
        "b_in": np.ascontiguousarray(inputs["L0_attn_b_in"]),
        "Dv": np.ascontiguousarray(inputs["L0_attn_D"]),
        "eye_bf": np.eye(128, dtype=ml_dtypes.bfloat16),
        "eye_f": np.eye(128, dtype=np.float32),
    }




def build_phase_b():
    nc = bass.Bass("TRN2", target_bir_lowering=False, detect_race_conditions=False)
    hb = {}
    hb["xmid"] = nc.dram_tensor("xmid", [NSEQ * SEQ, D], F32, kind="ExternalInput").ap()
    hb["m2_all"] = nc.dram_tensor("m2_all", [128, NBLK * NG], F32, kind="ExternalInput").ap()
    hb["r2_all"] = nc.dram_tensor("r2_all", [128, NBLK * NG], F32, kind="ExternalInput").ap()
    hb["maxes1"] = nc.dram_tensor("maxes1", [1, 8], F32, kind="ExternalInput").ap()
    hb["clip_x"] = nc.dram_tensor("clip_x", [1, 1], F32, kind="ExternalInput").ap()
    hb["w_gate"] = nc.dram_tensor("w_gate", [I, D], F32, kind="ExternalInput").ap()
    hb["w_up"] = nc.dram_tensor("w_up", [I, D], F32, kind="ExternalInput").ap()
    hb["eye_bf"] = nc.dram_tensor("eye_bf", [128, 128], BF16, kind="ExternalInput").ap()
    hb["t_out"] = nc.dram_tensor("t_out", [NBLK * NKI * 128, TB], F32, kind="ExternalOutput").ap()
    hb["alpha_u_o"] = nc.dram_tensor("alpha_u_o", [128, NKI], F32, kind="ExternalOutput").ap()
    hb["max2"] = nc.dram_tensor("max2", [1, 1], F32, kind="ExternalOutput").ap()

    es = contextlib.ExitStack()
    E = es.enter_context
    sb = {}

    def S_(name, shape, dtype):
        sb[name] = E(nc.sbuf_tensor(name, shape, dtype))

    S_("xa", [128, 2 * NG * D], F32)
    S_("v_scr", [128, NG * D], F32)
    S_("kx_tm", [128, NG * D], BF16)
    S_("kx_f", [128, NKD * TB], BF16)
    S_("wstage", [128, 2 * D], F32)
    S_("kwg_f", [128, NKD * I], BF16)
    S_("kwu_f", [128, NKD * I], BF16)
    S_("kq", [128, D], BF16)
    S_("wq_scr", [128, D], F32)
    S_("aw", [128, 1], F32)
    S_("wrec", [128, 1], F32)
    S_("swg", [128, NKI], F32)
    S_("swu", [128, NKI], F32)
    S_("alpha_g", [128, NKI], F32)
    S_("alpha_u", [128, NKI], F32)
    S_("q_all", [128, NBLK * NG], F32)
    S_("m2_all_s", [128, NBLK * NG], F32)
    S_("r2_all_s", [128, NBLK * NG], F32)
    S_("sg_scr", [128, 2 * TB], F32)
    S_("t_scr", [128, 2 * TB], F32)
    S_("rmax", [128, 1], F32)
    S_("macc2", [128, 1], F32)
    S_("mrow", [1, 128], F32)
    S_("max2_s", [1, 1], F32)
    S_("m1row", [1, 8], F32)
    S_("clipx", [1, 1], F32)
    S_("sigx", [1, 1], F32)
    S_("sxv", [1, 1], F32)
    S_("sxiv", [1, 1], F32)
    S_("sx_bc", [128, 1], F32)
    S_("sxi_bc", [128, 1], F32)
    S_("eye_bf_s", [128, 128], BF16)
    S_("eye_f_s", [128, 128], F32)

    ps = {}
    ps["pT0"] = E(nc.psum_tensor("pT0", [128, 1024], BF16))
    ps["pT1"] = E(nc.psum_tensor("pT1", [128, 1024], BF16))
    ps["pg"] = E(nc.psum_tensor("pg", [128, 2 * TB], F32))
    ps["pu2"] = E(nc.psum_tensor("pu2", [128, 2 * TB], F32))
    ps["pm"] = E(nc.psum_tensor("pm", [1, 128], F32))

    sems = {}
    for s in ["s_x", "s_kx", "s_tT", "s_tev", "s_gmm", "s_umm2", "s_silu", "s_tv",
              "s_ts", "s_wld", "s_wq", "s_wt", "s_wev", "s_vs", "s_small", "s_sig",
              "s_bc1", "s_dbb2", "s_pre", "s_fin", "s_st"]:
        sems[s] = E(nc.semaphore(s))
    lo = min(h.num for h in sems.values())
    hi = max(h.num for h in sems.values())
    g = sems
    pT = [ps["pT0"], ps["pT1"]]
    NW = 2 * NKI

    with nc.Block() as blk:
        @blk.sync
        def _(sync):
            for name, src in [("m2_all_s", hb["m2_all"]), ("r2_all_s", hb["r2_all"]),
                              ("m1row", hb["maxes1"]), ("clipx", hb["clip_x"]),
                              ("eye_bf_s", hb["eye_bf"])]:
                sync.dma_start(out=sb[name][:], in_=src).then_inc(g["s_small"], 16)
            for wi in range(NW):
                w = hb["w_gate"] if wi < NKI else hb["w_up"]
                i0 = (wi % NKI) * 128
                if wi >= 1:
                    sync.wait_ge(g["s_wld"], 16 * wi)
                if wi >= 2:
                    sync.wait_ge(g["s_wq"], wi - 1)
                sync.dma_start(out=sb["wstage"][:, (wi % 2) * D:(wi % 2 + 1) * D],
                               in_=w[i0:i0 + 128, :]).then_inc(g["s_wld"], 16)
            sync.wait_ge(g["s_bc1"], 1)
            _bcast_col(sync, sb["sx_bc"], 1, g["s_dbb2"], 0)
            _bcast_col(sync, sb["sxi_bc"], 1, g["s_dbb2"], 16 * 7)
            sync.nop().then_inc(g["s_dbb2"], 1)
            sync.wait_ge(g["s_pre"], 1)
            sync.dma_start(out=hb["alpha_u_o"], in_=sb["alpha_u"][:]).then_inc(g["s_st"], 16)
            for j in range(NBLK):
                if j >= 1:
                    sync.wait_ge(g["s_x"], 16 * j)
                if j >= 2:
                    sync.wait_ge(g["s_kx"], j - 1)
                sync.dma_start(out=_g3(sb["xa"][:, (j % 2) * NG * D:(j % 2 + 1) * NG * D]),
                               in_=_tm(hb["xmid"][j * TB:(j + 1) * TB, :])).then_inc(g["s_x"], 16)
            sync.wait_ge(g["s_fin"], 3)
            sync.dma_start(out=hb["max2"], in_=sb["max2_s"][:]).then_inc(g["s_st"], 16)
            sync.wait_ge(g["s_st"], 32)
            sync.wait_ge(g["s_ts"], 16 * NBLK * NKI)

        @blk.gpsimd
        def _(gpsimd):
            for j in range(NBLK):
                for m in range(NKI):
                    mh = j * NKI + m
                    gpsimd.wait_ge(g["s_tv"], mh + 1)
                    if mh >= 1:
                        gpsimd.wait_ge(g["s_ts"], 16 * mh)
                    gpsimd.dma_start(out=hb["t_out"][mh * 128:(mh + 1) * 128, :],
                                     in_=sb["t_scr"][:, (mh % 2) * TB:(mh % 2 + 1) * TB]).then_inc(g["s_ts"], 16)

        @blk.scalar
        def _(scalar):
            scalar.wait_ge(g["s_small"], 16 * 5)
            scalar.activation(sb["sigx"][:], sb["clipx"][:], AF.Sigmoid).then_inc(g["s_sig"], 1)
            for wi in range(NW):
                scalar.wait_ge(g["s_wt"], wi + 1)
                dst = sb["kwg_f"] if wi < NKI else sb["kwu_f"]
                m = wi % NKI
                for k in range(NKD):
                    ins = scalar.copy(dst[:, k * I + m * 128:k * I + (m + 1) * 128],
                                      pT[wi % 2][:, k * 128:(k + 1) * 128])
                    if k == NKD - 1:
                        ins.then_inc(g["s_wev"], 1)
            for j in range(NBLK):
                for k in range(NKD):
                    scalar.wait_ge(g["s_tT"], NKD * j + k + 1)
                    scalar.copy(sb["kx_f"][:, k * TB:(k + 1) * TB],
                                pT[k % 2][:, 0:TB]).then_inc(g["s_tev"], 1)
                for m in range(NKI):
                    mh = j * NKI + m
                    scalar.wait_ge(g["s_gmm"], mh + 1)
                    scalar.activation(sb["sg_scr"][:, (mh % 2) * TB:(mh % 2 + 1) * TB],
                                      ps["pg"][:, (mh % 2) * TB:(mh % 2 + 1) * TB],
                                      AF.Silu, scale=sb["alpha_g"][:, m:m + 1]).then_inc(g["s_silu"], 1)

        @blk.tensor
        def _(tensor):
            tensor.wait_ge(g["s_small"], 16 * 5)
            for wi in range(NW):
                tensor.wait_ge(g["s_wq"], wi + 1)
                if wi >= 2:
                    tensor.wait_ge(g["s_wev"], wi - 1)
                for k in range(NKD):
                    ins = tensor.transpose(pT[wi % 2][:, k * 128:(k + 1) * 128],
                                           sb["kq"][:, k * 128:(k + 1) * 128],
                                           sb["eye_bf_s"][:])
                    if k == NKD - 1:
                        ins.then_inc(g["s_wt"], 1)
            for j in range(NBLK):
                tensor.wait_ge(g["s_kx"], j + 1)
                for k in range(NKD):
                    if NKD * j + k >= 2:
                        tensor.wait_ge(g["s_tev"], NKD * j + k - 1)
                    elif j == 0 and k < 2:
                        tensor.wait_ge(g["s_wev"], NW)
                    for gg in range(NG):
                        ins = tensor.transpose(pT[k % 2][:, gg * 128:(gg + 1) * 128],
                                               sb["kx_tm"][:, gg * D + k * 128:gg * D + (k + 1) * 128],
                                               sb["eye_bf_s"][:])
                        if gg == NG - 1:
                            ins.then_inc(g["s_tT"], 1)
                tensor.wait_ge(g["s_tev"], NKD * j + NKD)
                tensor.wait_ge(g["s_pre"], 1)
                for m in range(NKI):
                    mh = j * NKI + m
                    if mh >= 2:
                        tensor.wait_ge(g["s_silu"], mh - 1)
                        tensor.wait_ge(g["s_tv"], mh - 1)
                    for k in range(NKD):
                        ins = tensor.matmul(ps["pg"][:, (mh % 2) * TB:(mh % 2 + 1) * TB],
                                            sb["kwg_f"][:, k * I + m * 128:k * I + (m + 1) * 128],
                                            sb["kx_f"][:, k * TB:(k + 1) * TB],
                                            start=(k == 0), stop=(k == NKD - 1))
                        if k == NKD - 1:
                            ins.then_inc(g["s_gmm"], 1)
                    for k in range(NKD):
                        ins = tensor.matmul(ps["pu2"][:, (mh % 2) * TB:(mh % 2 + 1) * TB],
                                            sb["kwu_f"][:, k * I + m * 128:k * I + (m + 1) * 128],
                                            sb["kx_f"][:, k * TB:(k + 1) * TB],
                                            start=(k == 0), stop=(k == NKD - 1))
                        if k == NKD - 1:
                            ins.then_inc(g["s_umm2"], 1)
            tensor.wait_ge(g["s_fin"], 1)
            tensor.transpose(ps["pm"][0:1, :], sb["macc2"][:],
                             sb["eye_f_s"][:]).then_inc(g["s_fin"], 1)

        @blk.vector
        def _(vector):
            vsn = [0]

            def vq(ins):
                vsn[0] += 1
                ins.then_inc(g["s_vs"], 1)
                vector.wait_ge(g["s_vs"], vsn[0])

            vector.wait_ge(g["s_small"], 16 * 5)
            vq(vector.tensor_copy(sb["eye_f_s"][:], sb["eye_bf_s"][:]))
            for wi in range(NW):
                vector.wait_ge(g["s_wld"], 16 * (wi + 1))
                if wi >= 1:
                    vector.wait_ge(g["s_wt"], wi)
                sw = sb["swg"] if wi < NKI else sb["swu"]
                m = wi % NKI
                buf = sb["wstage"][:, (wi % 2) * D:(wi % 2 + 1) * D]
                vq(vector.tensor_reduce(sb["aw"][:], buf, AX, OP.max, apply_absolute_value=True))
                vq(vector.tensor_scalar(sw[:, m:m + 1], sb["aw"][:], 1.0 / 127.0, None, OP.mult))
                vq(vector.reciprocal(sb["wrec"][:], sw[:, m:m + 1]))
                vq(vector.tensor_scalar(sb["wq_scr"][:], buf, sb["wrec"][:], None, OP.mult))
                vector.tensor_scalar(sb["kq"][:], sb["wq_scr"][:], MAGIC, MAGIC,
                                     OP.add, OP.subtract).then_inc(g["s_wq"], 1)
                vector.wait_ge(g["s_wq"], wi + 1)
            vector.wait_ge(g["s_sig"], 1)
            vq(vector.tensor_reduce(sb["sxv"][:], sb["m1row"][:], AX, OP.max))
            vq(vector.tensor_tensor(sb["sxv"][:], sb["sxv"][:], sb["sigx"][:], OP.mult))
            vq(vector.tensor_scalar(sb["sxv"][:], sb["sxv"][:], 1.0 / 127.0, None, OP.mult))
            vq(vector.reciprocal(sb["sxiv"][:], sb["sxv"][:]))
            vq(vector.tensor_copy(sb["sx_bc"][0:1, :], sb["sxv"][:]))
            vector.tensor_copy(sb["sxi_bc"][0:1, :], sb["sxiv"][:]).then_inc(g["s_bc1"], 1)
            vector.wait_ge(g["s_dbb2"], 16 * 14 + 1)
            vector.wait_ge(g["s_wev"], NW)
            vq(vector.tensor_scalar(sb["alpha_g"][:], sb["swg"][:], sb["sx_bc"][:], None, OP.mult))
            vq(vector.tensor_scalar(sb["alpha_u"][:], sb["swu"][:], sb["sx_bc"][:], None, OP.mult))
            vq(vector.tensor_scalar(sb["q_all"][:], sb["r2_all_s"][:], sb["sxi_bc"][:], None, OP.mult))
            vector.memset(sb["macc2"][:], 0.0).then_inc(g["s_pre"], 1)
            vector.wait_ge(g["s_pre"], 1)
            for j in range(NBLK):
                o = (j % 2) * NG * D
                vector.wait_ge(g["s_x"], 16 * (j + 1))
                for gg in range(NG):
                    idx = j * NG + gg
                    vq(vector.tensor_scalar(sb["v_scr"][:, gg * D:(gg + 1) * D],
                                            sb["xa"][:, o + gg * D:o + (gg + 1) * D],
                                            sb["m2_all_s"][:, idx:idx + 1],
                                            sb["q_all"][:, idx:idx + 1],
                                            OP.subtract, OP.mult))
                    vq(vector.tensor_scalar(sb["v_scr"][:, gg * D:(gg + 1) * D],
                                            sb["v_scr"][:, gg * D:(gg + 1) * D],
                                            -128.0, 127.0, OP.max, OP.min))
                    ins = vector.tensor_scalar(sb["kx_tm"][:, gg * D:(gg + 1) * D],
                                               sb["v_scr"][:, gg * D:(gg + 1) * D],
                                               MAGIC, MAGIC, OP.add, OP.subtract)
                    if gg == NG - 1:
                        ins.then_inc(g["s_kx"], 1)
                    else:
                        vq(ins)
                for m in range(NKI):
                    mh = j * NKI + m
                    vector.wait_ge(g["s_silu"], mh + 1)
                    vector.wait_ge(g["s_umm2"], mh + 1)
                    if mh >= 2:
                        vector.wait_ge(g["s_ts"], 16 * (mh - 1))
                    vector.tensor_tensor(sb["t_scr"][:, (mh % 2) * TB:(mh % 2 + 1) * TB],
                                         sb["sg_scr"][:, (mh % 2) * TB:(mh % 2 + 1) * TB],
                                         ps["pu2"][:, (mh % 2) * TB:(mh % 2 + 1) * TB],
                                         OP.mult).then_inc(g["s_tv"], 1)
                    vector.wait_ge(g["s_tv"], mh + 1)
                    vq(vector.tensor_reduce(sb["rmax"][:], sb["t_scr"][:, (mh % 2) * TB:(mh % 2 + 1) * TB],
                                            AX, OP.max, apply_absolute_value=True))
                    vq(vector.tensor_scalar(sb["rmax"][:], sb["rmax"][:],
                                            sb["alpha_u"][:, m:m + 1], None, OP.mult))
                    vq(vector.tensor_tensor(sb["macc2"][:], sb["macc2"][:], sb["rmax"][:], OP.max))
            vector.nop().then_inc(g["s_fin"], 1)
            vector.wait_ge(g["s_fin"], 2)
            vq(vector.tensor_copy(sb["mrow"][:], ps["pm"][0:1, :]))
            vector.tensor_reduce(sb["max2_s"][:], sb["mrow"][:], AX, OP.max).then_inc(g["s_fin"], 1)

    nc.gpsimd.dma_reset(range(lo, hi + 1))
    nc.gpsimd.sem_clear(range(lo, hi + 1))
    es.close()
    return nc


def build_phase_c():
    nc = bass.Bass("TRN2", target_bir_lowering=False, detect_race_conditions=False)
    hb = {}
    hb["xmid"] = nc.dram_tensor("xmid", [NSEQ * SEQ, D], F32, kind="ExternalInput").ap()
    hb["t_in"] = nc.dram_tensor("t_in", [NBLK * NKI * 128, TB], F32, kind="ExternalInput").ap()
    hb["alpha_u_i"] = nc.dram_tensor("alpha_u_i", [128, NKI], F32, kind="ExternalInput").ap()
    hb["maxes2"] = nc.dram_tensor("maxes2", [1, 8], F32, kind="ExternalInput").ap()
    hb["clip_down"] = nc.dram_tensor("clip_down", [1, 1], F32, kind="ExternalInput").ap()
    hb["w_down"] = nc.dram_tensor("w_down", [D, I], F32, kind="ExternalInput").ap()
    hb["eye_bf"] = nc.dram_tensor("eye_bf", [128, 128], BF16, kind="ExternalInput").ap()
    hb["out"] = nc.dram_tensor("out", [NSEQ * SEQ, D], F32, kind="ExternalOutput").ap()

    es = contextlib.ExitStack()
    E = es.enter_context
    sb = {}

    def S_(name, shape, dtype):
        sb[name] = E(nc.sbuf_tensor(name, shape, dtype))

    S_("xa", [128, 2 * NG * D], F32)
    S_("tb", [128, 2 * TB], F32)
    S_("kh_f", [128, NKI * TB], BF16)
    S_("kwd_f", [128, NKI * D], BF16)
    S_("wstage", [128, I], F32)
    S_("kq_d", [128, I], BF16)
    S_("wq_scr", [128, I], F32)
    S_("deq", [128, I], BF16)
    S_("aw", [128, 1], F32)
    S_("wrec", [128, 1], F32)
    S_("swd", [128, NKD], F32)
    S_("alpha_u", [128, NKI], F32)
    S_("c2", [128, NKI], F32)
    S_("m2row", [1, 8], F32)
    S_("clipd", [1, 1], F32)
    S_("sigd", [1, 1], F32)
    S_("shv", [1, 1], F32)
    S_("shiv", [1, 1], F32)
    S_("sh_bc", [128, 1], F32)
    S_("shi_bc", [128, 1], F32)
    S_("v_chunk", [128, TB], F32)
    S_("d_scr2", [128, D], F32)
    S_("out_s", [128, NG * D], F32)
    S_("eye_bf_s", [128, 128], BF16)

    ps = {}
    ps["pT0"] = E(nc.psum_tensor("pT0", [128, 1024], BF16))
    ps["pT1"] = E(nc.psum_tensor("pT1", [128, 1024], BF16))
    ps["pd"] = E(nc.psum_tensor("pd", [128, D], F32))

    sems = {}
    for s in ["s_x", "s_tl", "s_tv2", "s_kh", "s_dmm", "s_ofree", "s_wld", "s_wq",
              "s_wt", "s_wev", "s_vs", "s_small", "s_sig", "s_bc1", "s_dbb2",
              "s_pre", "s_st"]:
        sems[s] = E(nc.semaphore(s))
    lo = min(h.num for h in sems.values())
    hi = max(h.num for h in sems.values())
    g = sems
    pT = [ps["pT0"], ps["pT1"]]

    with nc.Block() as blk:
        @blk.sync
        def _(sync):
            for name, src in [("alpha_u", hb["alpha_u_i"]), ("m2row", hb["maxes2"]),
                              ("clipd", hb["clip_down"]), ("eye_bf_s", hb["eye_bf"])]:
                sync.dma_start(out=sb[name][:], in_=src).then_inc(g["s_small"], 16)
            for wi in range(NKD):
                if wi >= 1:
                    sync.wait_ge(g["s_wld"], 16 * wi)
                    sync.wait_ge(g["s_wq"], wi)
                sync.dma_start(out=sb["wstage"][:],
                               in_=hb["w_down"][wi * 128:(wi + 1) * 128, :]).then_inc(g["s_wld"], 16)
            sync.wait_ge(g["s_bc1"], 1)
            _bcast_col(sync, sb["sh_bc"], 1, g["s_dbb2"], 0)
            _bcast_col(sync, sb["shi_bc"], 1, g["s_dbb2"], 16 * 7)
            sync.nop().then_inc(g["s_dbb2"], 1)
            for j in range(NBLK):
                if j >= 1:
                    sync.wait_ge(g["s_x"], 16 * j)
                if j >= 2:
                    sync.wait_ge(g["s_ofree"], 4 * (j - 1))
                sync.dma_start(out=_g3(sb["xa"][:, (j % 2) * NG * D:(j % 2 + 1) * NG * D]),
                               in_=_tm(hb["xmid"][j * TB:(j + 1) * TB, :])).then_inc(g["s_x"], 16)
                if j >= 1:
                    sync.wait_ge(g["s_ofree"], 4 * j)
                    if j >= 2:
                        sync.wait_ge(g["s_st"], 16 * (j - 1))
                    sync.dma_start(out=_tm(hb["out"][(j - 1) * TB:j * TB, :]),
                                   in_=_g3(sb["out_s"][:])).then_inc(g["s_st"], 16)
            sync.wait_ge(g["s_ofree"], 4 * NBLK)
            sync.wait_ge(g["s_st"], 16 * (NBLK - 1))
            sync.dma_start(out=_tm(hb["out"][(NBLK - 1) * TB:NBLK * TB, :]),
                           in_=_g3(sb["out_s"][:])).then_inc(g["s_st"], 16)
            sync.wait_ge(g["s_st"], 16 * NBLK)

        @blk.gpsimd
        def _(gpsimd):
            for j in range(NBLK):
                for m in range(NKI):
                    mh = j * NKI + m
                    if mh >= 1:
                        gpsimd.wait_ge(g["s_tl"], 16 * mh)
                    if mh >= 2:
                        gpsimd.wait_ge(g["s_tv2"], mh - 1)
                    gpsimd.dma_start(out=sb["tb"][:, (mh % 2) * TB:(mh % 2 + 1) * TB],
                                     in_=hb["t_in"][mh * 128:(mh + 1) * 128, :]).then_inc(g["s_tl"], 16)

        @blk.scalar
        def _(scalar):
            scalar.wait_ge(g["s_small"], 16 * 4)
            scalar.activation(sb["sigd"][:], sb["clipd"][:], AF.Sigmoid).then_inc(g["s_sig"], 1)
            for wi in range(NKD):
                for m in range(NKI):
                    scalar.wait_ge(g["s_wt"], wi * NKI + m + 1)
                    scalar.copy(sb["kwd_f"][:, m * D + wi * 128:m * D + (wi + 1) * 128],
                                pT[m % 2][:, 0:128]).then_inc(g["s_wev"], 1)

        @blk.tensor
        def _(tensor):
            tensor.wait_ge(g["s_small"], 16 * 4)
            for wi in range(NKD):
                tensor.wait_ge(g["s_wq"], wi + 1)
                for m in range(NKI):
                    if wi * NKI + m >= 2:
                        tensor.wait_ge(g["s_wev"], wi * NKI + m - 1)
                    tensor.transpose(pT[m % 2][:, 0:128],
                                     sb["deq"][:, m * 128:(m + 1) * 128],
                                     sb["eye_bf_s"][:]).then_inc(g["s_wt"], 1)
            tensor.wait_ge(g["s_wev"], NKD * NKI)
            tensor.wait_ge(g["s_pre"], 1)
            for j in range(NBLK):
                tensor.wait_ge(g["s_kh"], j + 1)
                for gg in range(NG):
                    if 4 * j + gg >= 1:
                        tensor.wait_ge(g["s_ofree"], 4 * j + gg)
                    for m in range(NKI):
                        for hf in range(2):
                            ins = tensor.matmul(ps["pd"][:, hf * TB:(hf + 1) * TB],
                                                sb["kh_f"][:, m * TB + gg * 128:m * TB + (gg + 1) * 128],
                                                sb["kwd_f"][:, m * D + hf * TB:m * D + (hf + 1) * TB],
                                                start=(m == 0), stop=(m == NKI - 1))
                            if m == NKI - 1 and hf == 1:
                                ins.then_inc(g["s_dmm"], 1)

        @blk.vector
        def _(vector):
            vsn = [0]

            def vq(ins):
                vsn[0] += 1
                ins.then_inc(g["s_vs"], 1)
                vector.wait_ge(g["s_vs"], vsn[0])

            for wi in range(NKD):
                vector.wait_ge(g["s_wld"], 16 * (wi + 1))
                if wi >= 1:
                    vector.wait_ge(g["s_wt"], wi * NKI)
                vq(vector.tensor_reduce(sb["aw"][:], sb["wstage"][:], AX, OP.max,
                                        apply_absolute_value=True))
                vq(vector.tensor_scalar(sb["swd"][:, wi:wi + 1], sb["aw"][:], 1.0 / 127.0, None, OP.mult))
                vq(vector.reciprocal(sb["wrec"][:], sb["swd"][:, wi:wi + 1]))
                vq(vector.tensor_scalar(sb["wq_scr"][:], sb["wstage"][:], sb["wrec"][:], None, OP.mult))
                vq(vector.tensor_scalar(sb["wq_scr"][:], sb["wq_scr"][:], MAGIC, MAGIC, OP.add, OP.subtract))
                vector.tensor_scalar(sb["deq"][:], sb["wq_scr"][:], sb["swd"][:, wi:wi + 1],
                                     None, OP.mult).then_inc(g["s_wq"], 1)
                vector.wait_ge(g["s_wq"], wi + 1)
            vector.wait_ge(g["s_sig"], 1)
            vq(vector.tensor_reduce(sb["shv"][:], sb["m2row"][:], AX, OP.max))
            vq(vector.tensor_tensor(sb["shv"][:], sb["shv"][:], sb["sigd"][:], OP.mult))
            vq(vector.tensor_scalar(sb["shv"][:], sb["shv"][:], 1.0 / 127.0, None, OP.mult))
            vq(vector.reciprocal(sb["shiv"][:], sb["shv"][:]))
            vq(vector.tensor_copy(sb["sh_bc"][0:1, :], sb["shv"][:]))
            vector.tensor_copy(sb["shi_bc"][0:1, :], sb["shiv"][:]).then_inc(g["s_bc1"], 1)
            vector.wait_ge(g["s_dbb2"], 16 * 14 + 1)
            vq(vector.tensor_scalar(sb["c2"][:], sb["alpha_u"][:], sb["shi_bc"][:], None, OP.mult))
            vector.nop().then_inc(g["s_pre"], 1)
            for j in range(NBLK):
                o = (j % 2) * NG * D
                if j >= 1:
                    vector.wait_ge(g["s_dmm"], 4 * j)
                    vector.wait_ge(g["s_st"], 16 * j)
                for m in range(NKI):
                    mh = j * NKI + m
                    vector.wait_ge(g["s_tl"], 16 * (mh + 1))
                    vector.tensor_scalar(sb["v_chunk"][:], sb["tb"][:, (mh % 2) * TB:(mh % 2 + 1) * TB],
                                         sb["c2"][:, m:m + 1], None, OP.mult).then_inc(g["s_tv2"], 1)
                    vector.wait_ge(g["s_tv2"], mh + 1)
                    vq(vector.tensor_scalar(sb["v_chunk"][:], sb["v_chunk"][:],
                                            -128.0, 127.0, OP.max, OP.min))
                    ins = vector.tensor_scalar(sb["kh_f"][:, m * TB:(m + 1) * TB],
                                               sb["v_chunk"][:], MAGIC, MAGIC, OP.add, OP.subtract)
                    if m == NKI - 1:
                        ins.then_inc(g["s_kh"], 1)
                    else:
                        vq(ins)
                vector.wait_ge(g["s_x"], 16 * (j + 1))
                for gg in range(NG):
                    vector.wait_ge(g["s_dmm"], 4 * j + gg + 1)
                    vq(vector.tensor_scalar(sb["d_scr2"][:], ps["pd"][:], sb["sh_bc"][:], None, OP.mult))
                    vector.tensor_tensor(sb["out_s"][:, gg * D:(gg + 1) * D],
                                         sb["d_scr2"][:], sb["xa"][:, o + gg * D:o + (gg + 1) * D],
                                         OP.add).then_inc(g["s_ofree"], 1)
                    vector.wait_ge(g["s_ofree"], 4 * j + gg + 1)

    nc.gpsimd.dma_reset(range(lo, hi + 1))
    nc.gpsimd.sem_clear(range(lo, hi + 1))
    es.close()
    return nc


def _stage_phase_b(inputs, core, res_a):
    return {
        "xmid": res_a[core]["xmid"],
        "m2_all": res_a[core]["m2_all"],
        "r2_all": res_a[core]["r2_all"],
        "maxes1": np.stack([res_a[c]["max1"][0] for c in range(8)], axis=1).reshape(1, 8),
        "clip_x": inputs["L0_ffn_clip_x"].reshape(1, 1),
        "w_gate": np.ascontiguousarray(inputs["L0_ffn_w_gate"]),
        "w_up": np.ascontiguousarray(inputs["L0_ffn_w_up"]),
        "eye_bf": np.eye(128, dtype=ml_dtypes.bfloat16),
    }


def _stage_phase_c(inputs, core, res_a, res_b):
    return {
        "xmid": res_a[core]["xmid"],
        "t_in": res_b[core]["t_out"],
        "alpha_u_i": res_b[core]["alpha_u_o"],
        "maxes2": np.stack([res_b[c]["max2"][0] for c in range(8)], axis=1).reshape(1, 8),
        "clip_down": inputs["L0_ffn_clip_down"].reshape(1, 1),
        "w_down": np.ascontiguousarray(inputs["L0_ffn_w_down"]),
        "eye_bf": np.eye(128, dtype=ml_dtypes.bfloat16),
    }


def kernel(**inputs):
    nc_a = build_phase_a()
    res_a = run_bass_kernel_spmd(nc_a, [_stage_phase_a(inputs, c) for c in range(8)],
                                 core_ids=list(range(8))).results
    nc_b = build_phase_b()
    res_b = run_bass_kernel_spmd(nc_b, [_stage_phase_b(inputs, c, res_a) for c in range(8)],
                                 core_ids=list(range(8))).results
    nc_c = build_phase_c()
    res_c = run_bass_kernel_spmd(nc_c, [_stage_phase_c(inputs, c, res_a, res_b) for c in range(8)],
                                 core_ids=list(range(8))).results
    out = np.stack([res_c[c]["out"].reshape(NSEQ, SEQ, D) for c in range(8)])
    return out.reshape(16, SEQ, D)


# version 02
# speedup vs baseline: 1.0052x; 1.0052x over previous
import contextlib
import numpy as np
import ml_dtypes
import concourse.bass as bass
import concourse.mybir as mybir
from concourse.bass_utils import run_bass_kernel_spmd

F32 = mybir.dt.float32
BF16 = mybir.dt.bfloat16
OP = mybir.AluOpType
AF = mybir.ActivationFunctionType
AX = mybir.AxisListType.X

MAGIC = 1.5 * 2.0 ** 23
D = 1024
I = 2816
NKD = D // 128
NKI = I // 128
TB = 512
NG = TB // 128
SEQ = 4096
NSEQ = 2
NBLK = NSEQ * SEQ // TB
BPS = SEQ // TB
EPS = 1e-5
RD = 1.0 / D


def _tm(ap2d):
    return ap2d.rearrange("(g p) d -> p g d", p=128)


def _g3(sbuf_ap):
    return sbuf_ap.rearrange("p (g d) -> p g d", g=NG)


def _fm_vec(ap1d):
    return ap1d.rearrange("(k p) -> p k", p=128)


def _bcast_col(eng, tile, width, sem, base):
    p = 1
    n = 0
    while p < 128:
        eng.dma_start(out=tile[p:min(2 * p, 128), 0:width],
                      in_=tile[0:min(p, 128 - p), 0:width]).then_inc(sem, 16)
        n += 1
        eng.wait_ge(sem, base + 16 * n)
        p *= 2
    return n


def build_phase_a():
    nc = bass.Bass("TRN2", target_bir_lowering=False, detect_race_conditions=False)
    hb = {}
    hb["x"] = nc.dram_tensor("x", [NSEQ * SEQ, D], F32, kind="ExternalInput").ap()
    hb["W_in"] = nc.dram_tensor("W_in", [D, D], F32, kind="ExternalInput").ap()
    hb["C"] = nc.dram_tensor("C", [D, D], F32, kind="ExternalInput").ap()
    for nm in ["log_A", "B_vec", "b_in", "Dv"]:
        hb[nm] = nc.dram_tensor(nm, [D], F32, kind="ExternalInput").ap()
    hb["eye_bf"] = nc.dram_tensor("eye_bf", [128, 128], BF16, kind="ExternalInput").ap()
    hb["eye_f"] = nc.dram_tensor("eye_f", [128, 128], F32, kind="ExternalInput").ap()
    hb["xmid"] = nc.dram_tensor("xmid", [NSEQ * SEQ, D], F32, kind="ExternalOutput").ap()
    hb["m2_all"] = nc.dram_tensor("m2_all", [128, NBLK * NG], F32, kind="ExternalOutput").ap()
    hb["r2_all"] = nc.dram_tensor("r2_all", [128, NBLK * NG], F32, kind="ExternalOutput").ap()
    hb["max1"] = nc.dram_tensor("max1", [1, 1], F32, kind="ExternalOutput").ap()

    es = contextlib.ExitStack()
    E = es.enter_context
    sb = {}

    def S_(name, shape, dtype):
        sb[name] = E(nc.sbuf_tensor(name, shape, dtype))

    S_("xa", [128, 2 * NG * D], F32)
    S_("sq_scr", [128, NG * D], F32)
    S_("sq_scr2", [128, NG * D], F32)
    S_("xn_tm", [128, NG * D], BF16)
    S_("xn_f", [128, NKD * TB], BF16)
    S_("b_scan", [128, NKD * TB], BF16)
    S_("h", [128, 2 * NKD * TB], BF16)
    S_("xmid_s", [128, NG * D], F32)
    S_("d_scr", [128, D], F32)
    S_("W_bf", [128, NKD * D], BF16)
    S_("C_bf", [128, NKD * D], BF16)
    S_("wstage", [128, NKD * D], F32)
    S_("eye_bf_s", [128, 128], BF16)
    S_("eye_f_s", [128, 128], F32)
    S_("D_bc", [128, D], BF16)
    S_("logA_f", [128, NKD], F32)
    S_("a_f", [128, NKD], F32)
    S_("B_f", [128, NKD], F32)
    S_("bin_f", [128, NKD], F32)
    S_("Bb_f", [128, NKD], F32)
    S_("Dv_row", [1, D], F32)
    S_("st1", [128, 4 * NG], F32)
    S_("st2", [128, 4 * NG], F32)
    S_("m2_all_s", [128, NBLK * NG], F32)
    S_("r2_all_s", [128, NBLK * NG], F32)
    S_("macc", [128, 1], F32)
    S_("mrow", [1, 128], F32)
    S_("max1_s", [1, 1], F32)

    ps = {}
    ps["pT0"] = E(nc.psum_tensor("pT0", [128, 1024], BF16))
    ps["pT1"] = E(nc.psum_tensor("pT1", [128, 1024], BF16))
    ps["pu"] = E(nc.psum_tensor("pu", [128, 2 * TB], F32))
    ps["py"] = E(nc.psum_tensor("py", [128, D], F32))
    ps["pm"] = E(nc.psum_tensor("pm", [1, 128], F32))

    sems = {}
    for s in ["s_x", "s_sq1", "s_v1", "s_rstd", "s_xn", "s_tT", "s_tev", "s_bfree",
              "s_umm", "s_scan", "s_ymm", "s_yfree", "s_xm", "s_sq2", "s_v2",
              "s_rstd2", "s_xmdone", "s_st", "s_wv", "s_ww", "s_dbb", "s_wcast", "s_dbc", "s_fin", "s_vs"]:
        sems[s] = E(nc.semaphore(s))
    lo = min(h.num for h in sems.values())
    hi = max(h.num for h in sems.values())
    g = sems
    pT = [ps["pT0"], ps["pT1"]]

    with nc.Block() as blk:
        @blk.sync
        def _(sync):
            for name, src in [("logA_f", _fm_vec(hb["log_A"])), ("B_f", _fm_vec(hb["B_vec"])),
                              ("bin_f", _fm_vec(hb["b_in"])), ("Dv_row", hb["Dv"].unsqueeze(0)),
                              ("eye_bf_s", hb["eye_bf"]), ("eye_f_s", hb["eye_f"])]:
                sync.dma_start(out=sb[name][:], in_=src,
                               allow_slow_non_contiguous=True).then_inc(g["s_wv"], 16)
            sync.dma_start(out=sb["wstage"][:].rearrange("p (k d) -> p k d", k=NKD),
                           in_=hb["W_in"].rearrange("(k p) d -> p k d", p=128)).then_inc(g["s_ww"], 16)
            sync.wait_ge(g["s_wcast"], 1)
            sync.dma_start(out=sb["wstage"][:].rearrange("p (k d) -> p k d", k=NKD),
                           in_=hb["C"].rearrange("(k p) d -> p k d", p=128)).then_inc(g["s_ww"], 16)
            sync.wait_ge(g["s_dbc"], 1)
            _bcast_col(sync, sb["D_bc"], D, g["s_dbb"], 0)
            for j in range(NBLK):
                if j >= 2:
                    sync.wait_ge(g["s_xm"], 4 * (j - 2) + 4)
                if j >= 1:
                    sync.wait_ge(g["s_x"], 16 * j)
                sync.dma_start(out=_g3(sb["xa"][:, (j % 2) * NG * D:(j % 2 + 1) * NG * D]),
                               in_=_tm(hb["x"][j * TB:(j + 1) * TB, :])).then_inc(g["s_x"], 16)
                if j >= 1:
                    sync.wait_ge(g["s_xmdone"], j)
                    if j >= 2:
                        sync.wait_ge(g["s_st"], 16 * (j - 1))
                    sync.dma_start(out=_tm(hb["xmid"][(j - 1) * TB:j * TB, :]),
                                   in_=_g3(sb["xmid_s"][:])).then_inc(g["s_st"], 16)
            sync.wait_ge(g["s_xmdone"], NBLK)
            sync.wait_ge(g["s_st"], 16 * (NBLK - 1))
            sync.dma_start(out=_tm(hb["xmid"][(NBLK - 1) * TB:NBLK * TB, :]),
                           in_=_g3(sb["xmid_s"][:])).then_inc(g["s_st"], 16)
            sync.wait_ge(g["s_fin"], 3)
            sync.dma_start(out=hb["m2_all"], in_=sb["m2_all_s"][:]).then_inc(g["s_st"], 16)
            sync.dma_start(out=hb["r2_all"], in_=sb["r2_all_s"][:]).then_inc(g["s_st"], 16)
            sync.dma_start(out=hb["max1"], in_=sb["max1_s"][:]).then_inc(g["s_st"], 16)
            sync.wait_ge(g["s_st"], 16 * (NBLK + 3))

        @blk.scalar
        def _(scalar):
            scalar.wait_ge(g["s_ww"], 16)
            scalar.copy(sb["W_bf"][:], sb["wstage"][:]).then_inc(g["s_wcast"], 1)
            scalar.wait_ge(g["s_ww"], 32)
            scalar.copy(sb["C_bf"][:], sb["wstage"][:]).then_inc(g["s_wcast"], 1)
            scalar.wait_ge(g["s_wv"], 16 * 6)
            scalar.activation(sb["a_f"][:], sb["logA_f"][:], AF.Exp).then_inc(g["s_wcast"], 1)
            for j in range(NBLK):
                o = (j % 2) * NG * D
                scalar.wait_ge(g["s_x"], 16 * (j + 1))
                for gg in range(NG):
                    ins = scalar.activation(sb["sq_scr2"][:, gg * D:(gg + 1) * D],
                                            sb["xa"][:, o + gg * D:o + (gg + 1) * D],
                                            AF.Square, accum_out=sb["st1"][:, NG + gg:NG + gg + 1])
                    if gg == NG - 1:
                        ins.then_inc(g["s_sq1"], 1)
                scalar.wait_ge(g["s_v1"], j + 1)
                scalar.activation(sb["st1"][:, 3 * NG:4 * NG], sb["st1"][:, 2 * NG:3 * NG],
                                  AF.Sqrt).then_inc(g["s_rstd"], 1)
                for k in range(NKD):
                    scalar.wait_ge(g["s_tT"], NKD * j + k + 1)
                    scalar.copy(sb["xn_f"][:, k * TB:(k + 1) * TB],
                                pT[k % 2][:, 0:TB]).then_inc(g["s_tev"], 1)
                for gg in range(NG):
                    scalar.wait_ge(g["s_xm"], 4 * j + gg + 1)
                    ins = scalar.activation(sb["sq_scr2"][:, gg * D:(gg + 1) * D],
                                            sb["xmid_s"][:, gg * D:(gg + 1) * D],
                                            AF.Square, accum_out=sb["st2"][:, NG + gg:NG + gg + 1])
                    if gg == NG - 1:
                        ins.then_inc(g["s_sq2"], 1)
                scalar.wait_ge(g["s_v2"], j + 1)
                scalar.activation(sb["st2"][:, 3 * NG:4 * NG], sb["st2"][:, 2 * NG:3 * NG],
                                  AF.Sqrt).then_inc(g["s_rstd2"], 1)

        @blk.tensor
        def _(tensor):
            tensor.wait_ge(g["s_wv"], 16 * 6)
            for j in range(NBLK):
                tensor.wait_ge(g["s_xn"], j + 1)
                for k in range(NKD):
                    if NKD * j + k >= 2:
                        tensor.wait_ge(g["s_tev"], NKD * j + k - 1)
                    for gg in range(NG):
                        ins = tensor.transpose(pT[k % 2][:, gg * 128:(gg + 1) * 128],
                                               sb["xn_tm"][:, gg * D + k * 128:gg * D + (k + 1) * 128],
                                               sb["eye_bf_s"][:])
                        if gg == NG - 1:
                            ins.then_inc(g["s_tT"], 1)
                tensor.wait_ge(g["s_tev"], NKD * j + NKD)
                for m in range(NKD):
                    if NKD * j + m >= 2:
                        tensor.wait_ge(g["s_bfree"], NKD * j + m - 1)
                    for k in range(NKD):
                        ins = tensor.matmul(ps["pu"][:, (m % 2) * TB:(m % 2 + 1) * TB],
                                            sb["W_bf"][:, k * D + m * 128:k * D + (m + 1) * 128],
                                            sb["xn_f"][:, k * TB:(k + 1) * TB],
                                            start=(k == 0), stop=(k == NKD - 1))
                        if k == NKD - 1:
                            ins.then_inc(g["s_umm"], 1)
                tensor.wait_ge(g["s_scan"], j + 1)
                for gg in range(NG):
                    if 4 * j + gg >= 1:
                        tensor.wait_ge(g["s_yfree"], 4 * j + gg)
                    for k in range(NKD):
                        for hf in range(2):
                            ins = tensor.matmul(ps["py"][:, hf * TB:(hf + 1) * TB],
                                                sb["h"][:, (j % 2) * NKD * TB + k * TB + gg * 128:
                                                        (j % 2) * NKD * TB + k * TB + (gg + 1) * 128],
                                                sb["C_bf"][:, k * D + hf * TB:k * D + (hf + 1) * TB],
                                                start=(k == 0), stop=(k == NKD - 1))
                            if k == NKD - 1 and hf == 1:
                                ins.then_inc(g["s_ymm"], 1)
            tensor.wait_ge(g["s_fin"], 1)
            tensor.transpose(ps["pm"][0:1, :], sb["macc"][:],
                             sb["eye_f_s"][:]).then_inc(g["s_fin"], 1)

        @blk.vector
        def _(vector):
            vsn = [0]

            def vq(ins):
                vsn[0] += 1
                ins.then_inc(g["s_vs"], 1)
                vector.wait_ge(g["s_vs"], vsn[0])
            vector.wait_ge(g["s_wcast"], 3)
            vq(vector.tensor_scalar(sb["a_f"][:], sb["a_f"][:], -1.0, None, OP.mult))
            vq(vector.tensor_tensor(sb["Bb_f"][:], sb["B_f"][:], sb["bin_f"][:], OP.mult))
            vq(vector.memset(sb["macc"][:], 0.0))
            vector.tensor_copy(sb["D_bc"][0:1, :], sb["Dv_row"][0:1, :]).then_inc(g["s_dbc"], 1)
            vector.wait_ge(g["s_dbb"], 16 * 7)
            for j in range(NBLK):
                o = (j % 2) * NG * D
                vector.wait_ge(g["s_x"], 16 * (j + 1))
                for gg in range(NG):
                    ins = vector.tensor_scalar(sb["sq_scr"][:, gg * D:(gg + 1) * D],
                                               sb["xa"][:, o + gg * D:o + (gg + 1) * D],
                                               1.0, 0.0, OP.mult, OP.add,
                                               accum_out=sb["st1"][:, gg:gg + 1])
                    ins.then_inc(g["s_vs"], 1)
                vsn[0] += NG
                vector.wait_ge(g["s_vs"], vsn[0])
                vector.wait_ge(g["s_sq1"], j + 1)
                vq(vector.tensor_scalar(sb["st1"][:, 0:NG], sb["st1"][:, 0:NG], RD, None, OP.mult))
                vq(vector.tensor_scalar(sb["st1"][:, NG:2 * NG], sb["st1"][:, NG:2 * NG], RD, None, OP.mult))
                vq(vector.tensor_tensor(sb["st1"][:, 2 * NG:3 * NG], sb["st1"][:, 0:NG], sb["st1"][:, 0:NG], OP.mult))
                vq(vector.tensor_tensor(sb["st1"][:, 2 * NG:3 * NG], sb["st1"][:, NG:2 * NG],
                                        sb["st1"][:, 2 * NG:3 * NG], OP.subtract))
                vq(vector.tensor_scalar(sb["st1"][:, 2 * NG:3 * NG], sb["st1"][:, 2 * NG:3 * NG], EPS, None, OP.add))
                vector.reciprocal(sb["st1"][:, 2 * NG:3 * NG],
                                  sb["st1"][:, 2 * NG:3 * NG]).then_inc(g["s_v1"], 1)
                vector.wait_ge(g["s_rstd"], j + 1)
                for gg in range(NG):
                    ins = vector.tensor_scalar(sb["xn_tm"][:, gg * D:(gg + 1) * D],
                                               sb["xa"][:, o + gg * D:o + (gg + 1) * D],
                                               sb["st1"][:, gg:gg + 1],
                                               sb["st1"][:, 3 * NG + gg:3 * NG + gg + 1],
                                               OP.subtract, OP.mult)
                    if gg == NG - 1:
                        ins.then_inc(g["s_xn"], 1)
                for m in range(NKD):
                    vector.wait_ge(g["s_umm"], NKD * j + m + 1)
                    vq(vector.tensor_scalar(sb["b_scan"][:, m * TB:(m + 1) * TB],
                                            ps["pu"][:, (m % 2) * TB:(m % 2 + 1) * TB],
                                            sb["B_f"][:, m:m + 1], sb["Bb_f"][:, m:m + 1],
                                            OP.mult, OP.add))
                    vector.nop().then_inc(g["s_bfree"], 1)
                for k in range(NKD):
                    first = (j % BPS == 0)
                    init = 0.0 if first else sb["h"][:, ((j - 1) % 2) * NKD * TB + (k + 1) * TB - 1:
                                                     ((j - 1) % 2) * NKD * TB + (k + 1) * TB]
                    vq(vector.tensor_tensor_scan(
                        sb["h"][:, (j % 2) * NKD * TB + k * TB:(j % 2) * NKD * TB + (k + 1) * TB],
                        sb["a_f"][:, k:k + 1].broadcast_to((128, TB)),
                        sb["b_scan"][:, k * TB:(k + 1) * TB],
                        init, OP.mult, OP.add))
                    vector.drain()
                vector.nop().then_inc(g["s_scan"], 1)
                for gg in range(NG):
                    vector.wait_ge(g["s_ymm"], 4 * j + gg + 1)
                    vq(vector.tensor_tensor(sb["d_scr"][:], sb["xn_tm"][:, gg * D:(gg + 1) * D],
                                            sb["D_bc"][:], OP.mult))
                    vq(vector.tensor_tensor(sb["xmid_s"][:, gg * D:(gg + 1) * D],
                                            sb["xa"][:, o + gg * D:o + (gg + 1) * D], ps["py"][:], OP.add))
                    vector.tensor_tensor(sb["xmid_s"][:, gg * D:(gg + 1) * D],
                                         sb["xmid_s"][:, gg * D:(gg + 1) * D],
                                         sb["d_scr"][:], OP.add).then_inc(g["s_yfree"], 1)
                    vector.wait_ge(g["s_yfree"], 4 * j + gg + 1)
                    vector.tensor_scalar(sb["sq_scr"][:, gg * D:(gg + 1) * D],
                                         sb["xmid_s"][:, gg * D:(gg + 1) * D],
                                         1.0, 0.0, OP.mult, OP.add,
                                         accum_out=sb["st2"][:, gg:gg + 1]).then_inc(g["s_vs"], 1)
                    vsn[0] += 1
                    vector.nop().then_inc(g["s_xm"], 1)
                vector.wait_ge(g["s_vs"], vsn[0])
                vector.wait_ge(g["s_sq2"], j + 1)
                vq(vector.tensor_scalar(sb["st2"][:, 0:NG], sb["st2"][:, 0:NG], RD, None, OP.mult))
                vq(vector.tensor_scalar(sb["st2"][:, NG:2 * NG], sb["st2"][:, NG:2 * NG], RD, None, OP.mult))
                vq(vector.tensor_tensor(sb["st2"][:, 2 * NG:3 * NG], sb["st2"][:, 0:NG], sb["st2"][:, 0:NG], OP.mult))
                vq(vector.tensor_tensor(sb["st2"][:, 2 * NG:3 * NG], sb["st2"][:, NG:2 * NG],
                                        sb["st2"][:, 2 * NG:3 * NG], OP.subtract))
                vq(vector.tensor_scalar(sb["st2"][:, 2 * NG:3 * NG], sb["st2"][:, 2 * NG:3 * NG], EPS, None, OP.add))
                vector.reciprocal(sb["st2"][:, 2 * NG:3 * NG],
                                  sb["st2"][:, 2 * NG:3 * NG]).then_inc(g["s_v2"], 1)
                vector.wait_ge(g["s_rstd2"], j + 1)
                vector.tensor_copy(sb["m2_all_s"][:, j * NG:(j + 1) * NG], sb["st2"][:, 0:NG])
                vector.tensor_copy(sb["r2_all_s"][:, j * NG:(j + 1) * NG], sb["st2"][:, 3 * NG:4 * NG])
                for gg in range(NG):
                    vq(vector.tensor_scalar(sb["d_scr"][:], sb["xmid_s"][:, gg * D:(gg + 1) * D],
                                            sb["st2"][:, gg:gg + 1], None, OP.subtract))
                    vq(vector.tensor_reduce(sb["st2"][:, 2 * NG + gg:2 * NG + gg + 1], sb["d_scr"][:],
                                            AX, OP.max, apply_absolute_value=True))
                vq(vector.tensor_tensor(sb["st2"][:, 2 * NG:3 * NG], sb["st2"][:, 2 * NG:3 * NG],
                                        sb["st2"][:, 3 * NG:4 * NG], OP.mult))
                vq(vector.tensor_reduce(sb["d_scr"][:, 0:1], sb["st2"][:, 2 * NG:3 * NG], AX, OP.max))
                vector.tensor_tensor(sb["macc"][:], sb["macc"][:], sb["d_scr"][:, 0:1],
                                     OP.max).then_inc(g["s_xmdone"], 1)
                vector.wait_ge(g["s_xmdone"], j + 1)
                vector.wait_ge(g["s_st"], 16 * (j + 1))
            vector.nop().then_inc(g["s_fin"], 1)
            vector.wait_ge(g["s_fin"], 2)
            vq(vector.tensor_copy(sb["mrow"][:], ps["pm"][0:1, :]))
            vector.tensor_reduce(sb["max1_s"][:], sb["mrow"][:], AX, OP.max).then_inc(g["s_fin"], 1)

    nc.gpsimd.dma_reset(range(lo, hi + 1))
    nc.gpsimd.sem_clear(range(lo, hi + 1))
    es.close()
    return nc


def _stage_phase_a(inputs, core):
    x = inputs["L0_attn_x"]
    return {
        "x": np.ascontiguousarray(x[2 * core:2 * core + 2].reshape(NSEQ * SEQ, D)),
        "W_in": np.ascontiguousarray(inputs["L0_attn_W_in"]),
        "C": np.ascontiguousarray(inputs["L0_attn_C"]),
        "log_A": np.ascontiguousarray(inputs["L0_attn_log_A"]),
        "B_vec": np.ascontiguousarray(inputs["L0_attn_B_vec"]),
        "b_in": np.ascontiguousarray(inputs["L0_attn_b_in"]),
        "Dv": np.ascontiguousarray(inputs["L0_attn_D"]),
        "eye_bf": np.eye(128, dtype=ml_dtypes.bfloat16),
        "eye_f": np.eye(128, dtype=np.float32),
    }




def build_phase_b():
    nc = bass.Bass("TRN2", target_bir_lowering=False, detect_race_conditions=False)
    hb = {}
    hb["xmid"] = nc.dram_tensor("xmid", [NSEQ * SEQ, D], F32, kind="ExternalInput").ap()
    hb["m2_all"] = nc.dram_tensor("m2_all", [128, NBLK * NG], F32, kind="ExternalInput").ap()
    hb["r2_all"] = nc.dram_tensor("r2_all", [128, NBLK * NG], F32, kind="ExternalInput").ap()
    hb["maxes1"] = nc.dram_tensor("maxes1", [1, 8], F32, kind="ExternalInput").ap()
    hb["clip_x"] = nc.dram_tensor("clip_x", [1, 1], F32, kind="ExternalInput").ap()
    hb["w_gate"] = nc.dram_tensor("w_gate", [I, D], F32, kind="ExternalInput").ap()
    hb["w_up"] = nc.dram_tensor("w_up", [I, D], F32, kind="ExternalInput").ap()
    hb["eye_bf"] = nc.dram_tensor("eye_bf", [128, 128], BF16, kind="ExternalInput").ap()
    hb["t_out"] = nc.dram_tensor("t_out", [NBLK * NKI * 128, TB], F32, kind="ExternalOutput").ap()
    hb["alpha_u_o"] = nc.dram_tensor("alpha_u_o", [128, NKI], F32, kind="ExternalOutput").ap()
    hb["max2"] = nc.dram_tensor("max2", [1, 1], F32, kind="ExternalOutput").ap()

    es = contextlib.ExitStack()
    E = es.enter_context
    sb = {}

    def S_(name, shape, dtype):
        sb[name] = E(nc.sbuf_tensor(name, shape, dtype))

    S_("xa", [128, 2 * NG * D], F32)
    S_("v_scr", [128, NG * D], F32)
    S_("kx_tm", [128, NG * D], BF16)
    S_("kx_f", [128, NKD * TB], BF16)
    S_("wstage", [128, 2 * D], F32)
    S_("kwg_f", [128, NKD * I], BF16)
    S_("kwu_f", [128, NKD * I], BF16)
    S_("kq", [128, D], BF16)
    S_("wq_scr", [128, D], F32)
    S_("iq", [128, D], mybir.dt.int32)
    S_("aw", [128, 1], F32)
    S_("wrec", [128, 1], F32)
    S_("swg", [128, NKI], F32)
    S_("swu", [128, NKI], F32)
    S_("alpha_g", [128, NKI], F32)
    S_("alpha_u", [128, NKI], F32)
    S_("q_all", [128, NBLK * NG], F32)
    S_("m2_all_s", [128, NBLK * NG], F32)
    S_("r2_all_s", [128, NBLK * NG], F32)
    S_("sg_scr", [128, 2 * TB], F32)
    S_("t_scr", [128, 2 * TB], F32)
    S_("rmax", [128, 1], F32)
    S_("macc2", [128, 1], F32)
    S_("mrow", [1, 128], F32)
    S_("max2_s", [1, 1], F32)
    S_("m1row", [1, 8], F32)
    S_("clipx", [1, 1], F32)
    S_("sigx", [1, 1], F32)
    S_("sxv", [1, 1], F32)
    S_("sxiv", [1, 1], F32)
    S_("sx_bc", [128, 1], F32)
    S_("sxi_bc", [128, 1], F32)
    S_("eye_bf_s", [128, 128], BF16)
    S_("eye_f_s", [128, 128], F32)

    ps = {}
    ps["pT0"] = E(nc.psum_tensor("pT0", [128, 1024], BF16))
    ps["pT1"] = E(nc.psum_tensor("pT1", [128, 1024], BF16))
    ps["pg"] = E(nc.psum_tensor("pg", [128, 2 * TB], F32))
    ps["pu2"] = E(nc.psum_tensor("pu2", [128, 2 * TB], F32))
    ps["pm"] = E(nc.psum_tensor("pm", [1, 128], F32))

    sems = {}
    for s in ["s_x", "s_kx", "s_tT", "s_tev", "s_gmm", "s_umm2", "s_silu", "s_tv",
              "s_ts", "s_wld", "s_wq", "s_wt", "s_wev", "s_vs", "s_small", "s_sig",
              "s_bc1", "s_dbb2", "s_pre", "s_fin", "s_st"]:
        sems[s] = E(nc.semaphore(s))
    lo = min(h.num for h in sems.values())
    hi = max(h.num for h in sems.values())
    g = sems
    pT = [ps["pT0"], ps["pT1"]]
    NW = 2 * NKI

    with nc.Block() as blk:
        @blk.sync
        def _(sync):
            for name, src in [("m2_all_s", hb["m2_all"]), ("r2_all_s", hb["r2_all"]),
                              ("m1row", hb["maxes1"]), ("clipx", hb["clip_x"]),
                              ("eye_bf_s", hb["eye_bf"])]:
                sync.dma_start(out=sb[name][:], in_=src).then_inc(g["s_small"], 16)
            for wi in range(NW):
                w = hb["w_gate"] if wi < NKI else hb["w_up"]
                i0 = (wi % NKI) * 128
                if wi >= 1:
                    sync.wait_ge(g["s_wld"], 16 * wi)
                if wi >= 2:
                    sync.wait_ge(g["s_wq"], wi - 1)
                sync.dma_start(out=sb["wstage"][:, (wi % 2) * D:(wi % 2 + 1) * D],
                               in_=w[i0:i0 + 128, :]).then_inc(g["s_wld"], 16)
            sync.wait_ge(g["s_bc1"], 1)
            _bcast_col(sync, sb["sx_bc"], 1, g["s_dbb2"], 0)
            _bcast_col(sync, sb["sxi_bc"], 1, g["s_dbb2"], 16 * 7)
            sync.nop().then_inc(g["s_dbb2"], 1)
            sync.wait_ge(g["s_pre"], 1)
            sync.dma_start(out=hb["alpha_u_o"], in_=sb["alpha_u"][:]).then_inc(g["s_st"], 16)
            for j in range(NBLK):
                if j >= 1:
                    sync.wait_ge(g["s_x"], 16 * j)
                if j >= 2:
                    sync.wait_ge(g["s_kx"], j - 1)
                sync.dma_start(out=_g3(sb["xa"][:, (j % 2) * NG * D:(j % 2 + 1) * NG * D]),
                               in_=_tm(hb["xmid"][j * TB:(j + 1) * TB, :])).then_inc(g["s_x"], 16)
            sync.wait_ge(g["s_fin"], 3)
            sync.dma_start(out=hb["max2"], in_=sb["max2_s"][:]).then_inc(g["s_st"], 16)
            sync.wait_ge(g["s_st"], 32)
            sync.wait_ge(g["s_ts"], 16 * NBLK * NKI)

        @blk.gpsimd
        def _(gpsimd):
            for j in range(NBLK):
                for m in range(NKI):
                    mh = j * NKI + m
                    gpsimd.wait_ge(g["s_tv"], mh + 1)
                    if mh >= 1:
                        gpsimd.wait_ge(g["s_ts"], 16 * mh)
                    gpsimd.dma_start(out=hb["t_out"][mh * 128:(mh + 1) * 128, :],
                                     in_=sb["t_scr"][:, (mh % 2) * TB:(mh % 2 + 1) * TB]).then_inc(g["s_ts"], 16)

        @blk.scalar
        def _(scalar):
            scalar.wait_ge(g["s_small"], 16 * 5)
            scalar.activation(sb["sigx"][:], sb["clipx"][:], AF.Sigmoid).then_inc(g["s_sig"], 1)
            for wi in range(NW):
                scalar.wait_ge(g["s_wt"], wi + 1)
                dst = sb["kwg_f"] if wi < NKI else sb["kwu_f"]
                m = wi % NKI
                for k in range(NKD):
                    ins = scalar.copy(dst[:, k * I + m * 128:k * I + (m + 1) * 128],
                                      pT[wi % 2][:, k * 128:(k + 1) * 128])
                    if k == NKD - 1:
                        ins.then_inc(g["s_wev"], 1)
            for j in range(NBLK):
                for k in range(NKD):
                    scalar.wait_ge(g["s_tT"], NKD * j + k + 1)
                    scalar.copy(sb["kx_f"][:, k * TB:(k + 1) * TB],
                                pT[k % 2][:, 0:TB]).then_inc(g["s_tev"], 1)
                for m in range(NKI):
                    mh = j * NKI + m
                    scalar.wait_ge(g["s_gmm"], mh + 1)
                    scalar.activation(sb["sg_scr"][:, (mh % 2) * TB:(mh % 2 + 1) * TB],
                                      ps["pg"][:, (mh % 2) * TB:(mh % 2 + 1) * TB],
                                      AF.Silu, scale=sb["alpha_g"][:, m:m + 1]).then_inc(g["s_silu"], 1)

        @blk.tensor
        def _(tensor):
            tensor.wait_ge(g["s_small"], 16 * 5)
            for wi in range(NW):
                tensor.wait_ge(g["s_wq"], wi + 1)
                if wi >= 2:
                    tensor.wait_ge(g["s_wev"], wi - 1)
                for k in range(NKD):
                    ins = tensor.transpose(pT[wi % 2][:, k * 128:(k + 1) * 128],
                                           sb["kq"][:, k * 128:(k + 1) * 128],
                                           sb["eye_bf_s"][:])
                    if k == NKD - 1:
                        ins.then_inc(g["s_wt"], 1)
            for j in range(NBLK):
                tensor.wait_ge(g["s_kx"], j + 1)
                for k in range(NKD):
                    if NKD * j + k >= 2:
                        tensor.wait_ge(g["s_tev"], NKD * j + k - 1)
                    elif j == 0 and k < 2:
                        tensor.wait_ge(g["s_wev"], NW)
                    for gg in range(NG):
                        ins = tensor.transpose(pT[k % 2][:, gg * 128:(gg + 1) * 128],
                                               sb["kx_tm"][:, gg * D + k * 128:gg * D + (k + 1) * 128],
                                               sb["eye_bf_s"][:])
                        if gg == NG - 1:
                            ins.then_inc(g["s_tT"], 1)
                tensor.wait_ge(g["s_tev"], NKD * j + NKD)
                tensor.wait_ge(g["s_pre"], 1)
                for m in range(NKI):
                    mh = j * NKI + m
                    if mh >= 2:
                        tensor.wait_ge(g["s_silu"], mh - 1)
                        tensor.wait_ge(g["s_tv"], mh - 1)
                    for k in range(NKD):
                        ins = tensor.matmul(ps["pg"][:, (mh % 2) * TB:(mh % 2 + 1) * TB],
                                            sb["kwg_f"][:, k * I + m * 128:k * I + (m + 1) * 128],
                                            sb["kx_f"][:, k * TB:(k + 1) * TB],
                                            start=(k == 0), stop=(k == NKD - 1))
                        if k == NKD - 1:
                            ins.then_inc(g["s_gmm"], 1)
                    for k in range(NKD):
                        ins = tensor.matmul(ps["pu2"][:, (mh % 2) * TB:(mh % 2 + 1) * TB],
                                            sb["kwu_f"][:, k * I + m * 128:k * I + (m + 1) * 128],
                                            sb["kx_f"][:, k * TB:(k + 1) * TB],
                                            start=(k == 0), stop=(k == NKD - 1))
                        if k == NKD - 1:
                            ins.then_inc(g["s_umm2"], 1)
            tensor.wait_ge(g["s_fin"], 1)
            tensor.transpose(ps["pm"][0:1, :], sb["macc2"][:],
                             sb["eye_f_s"][:]).then_inc(g["s_fin"], 1)

        @blk.vector
        def _(vector):
            vsn = [0]

            def vq(ins):
                vsn[0] += 1
                ins.then_inc(g["s_vs"], 1)
                vector.wait_ge(g["s_vs"], vsn[0])

            vector.wait_ge(g["s_small"], 16 * 5)
            vq(vector.tensor_copy(sb["eye_f_s"][:], sb["eye_bf_s"][:]))
            for wi in range(NW):
                vector.wait_ge(g["s_wld"], 16 * (wi + 1))
                if wi >= 1:
                    vector.wait_ge(g["s_wt"], wi)
                sw = sb["swg"] if wi < NKI else sb["swu"]
                m = wi % NKI
                buf = sb["wstage"][:, (wi % 2) * D:(wi % 2 + 1) * D]
                vq(vector.tensor_reduce(sb["aw"][:], buf, AX, OP.max, apply_absolute_value=True))
                vq(vector.tensor_scalar(sw[:, m:m + 1], sb["aw"][:], 1.0 / 127.0, None, OP.mult))
                vq(vector.reciprocal(sb["wrec"][:], sw[:, m:m + 1]))
                vq(vector.tensor_scalar(sb["wq_scr"][:], buf, sb["wrec"][:], None, OP.mult))
                vq(vector.tensor_copy(sb["iq"][:], sb["wq_scr"][:]))
                vector.tensor_copy(sb["kq"][:], sb["iq"][:]).then_inc(g["s_wq"], 1)
                vector.wait_ge(g["s_wq"], wi + 1)
            vector.wait_ge(g["s_sig"], 1)
            vq(vector.tensor_reduce(sb["sxv"][:], sb["m1row"][:], AX, OP.max))
            vq(vector.tensor_tensor(sb["sxv"][:], sb["sxv"][:], sb["sigx"][:], OP.mult))
            vq(vector.tensor_scalar(sb["sxv"][:], sb["sxv"][:], 1.0 / 127.0, None, OP.mult))
            vq(vector.reciprocal(sb["sxiv"][:], sb["sxv"][:]))
            vq(vector.tensor_copy(sb["sx_bc"][0:1, :], sb["sxv"][:]))
            vector.tensor_copy(sb["sxi_bc"][0:1, :], sb["sxiv"][:]).then_inc(g["s_bc1"], 1)
            vector.wait_ge(g["s_dbb2"], 16 * 14 + 1)
            vector.wait_ge(g["s_wev"], NW)
            vq(vector.tensor_scalar(sb["alpha_g"][:], sb["swg"][:], sb["sx_bc"][:], None, OP.mult))
            vq(vector.tensor_scalar(sb["alpha_u"][:], sb["swu"][:], sb["sx_bc"][:], None, OP.mult))
            vq(vector.tensor_scalar(sb["q_all"][:], sb["r2_all_s"][:], sb["sxi_bc"][:], None, OP.mult))
            vector.memset(sb["macc2"][:], 0.0).then_inc(g["s_pre"], 1)
            vector.wait_ge(g["s_pre"], 1)
            for j in range(NBLK):
                o = (j % 2) * NG * D
                vector.wait_ge(g["s_x"], 16 * (j + 1))
                for gg in range(NG):
                    idx = j * NG + gg
                    vq(vector.tensor_scalar(sb["v_scr"][:, gg * D:(gg + 1) * D],
                                            sb["xa"][:, o + gg * D:o + (gg + 1) * D],
                                            sb["m2_all_s"][:, idx:idx + 1],
                                            sb["q_all"][:, idx:idx + 1],
                                            OP.subtract, OP.mult))
                    vq(vector.tensor_scalar(sb["v_scr"][:, gg * D:(gg + 1) * D],
                                            sb["v_scr"][:, gg * D:(gg + 1) * D],
                                            -128.0, 127.0, OP.max, OP.min))
                    vq(vector.tensor_copy(sb["iq"][:], sb["v_scr"][:, gg * D:(gg + 1) * D]))
                    ins = vector.tensor_copy(sb["kx_tm"][:, gg * D:(gg + 1) * D], sb["iq"][:])
                    if gg == NG - 1:
                        ins.then_inc(g["s_kx"], 1)
                    else:
                        vq(ins)
                for m in range(NKI):
                    mh = j * NKI + m
                    vector.wait_ge(g["s_silu"], mh + 1)
                    vector.wait_ge(g["s_umm2"], mh + 1)
                    if mh >= 2:
                        vector.wait_ge(g["s_ts"], 16 * (mh - 1))
                    vector.tensor_tensor(sb["t_scr"][:, (mh % 2) * TB:(mh % 2 + 1) * TB],
                                         sb["sg_scr"][:, (mh % 2) * TB:(mh % 2 + 1) * TB],
                                         ps["pu2"][:, (mh % 2) * TB:(mh % 2 + 1) * TB],
                                         OP.mult).then_inc(g["s_tv"], 1)
                    vector.wait_ge(g["s_tv"], mh + 1)
                    vq(vector.tensor_reduce(sb["rmax"][:], sb["t_scr"][:, (mh % 2) * TB:(mh % 2 + 1) * TB],
                                            AX, OP.max, apply_absolute_value=True))
                    vq(vector.tensor_scalar(sb["rmax"][:], sb["rmax"][:],
                                            sb["alpha_u"][:, m:m + 1], None, OP.mult))
                    vq(vector.tensor_tensor(sb["macc2"][:], sb["macc2"][:], sb["rmax"][:], OP.max))
            vector.nop().then_inc(g["s_fin"], 1)
            vector.wait_ge(g["s_fin"], 2)
            vq(vector.tensor_copy(sb["mrow"][:], ps["pm"][0:1, :]))
            vector.tensor_reduce(sb["max2_s"][:], sb["mrow"][:], AX, OP.max).then_inc(g["s_fin"], 1)

    nc.gpsimd.dma_reset(range(lo, hi + 1))
    nc.gpsimd.sem_clear(range(lo, hi + 1))
    es.close()
    return nc


def build_phase_c():
    nc = bass.Bass("TRN2", target_bir_lowering=False, detect_race_conditions=False)
    hb = {}
    hb["xmid"] = nc.dram_tensor("xmid", [NSEQ * SEQ, D], F32, kind="ExternalInput").ap()
    hb["t_in"] = nc.dram_tensor("t_in", [NBLK * NKI * 128, TB], F32, kind="ExternalInput").ap()
    hb["alpha_u_i"] = nc.dram_tensor("alpha_u_i", [128, NKI], F32, kind="ExternalInput").ap()
    hb["maxes2"] = nc.dram_tensor("maxes2", [1, 8], F32, kind="ExternalInput").ap()
    hb["clip_down"] = nc.dram_tensor("clip_down", [1, 1], F32, kind="ExternalInput").ap()
    hb["w_down"] = nc.dram_tensor("w_down", [D, I], F32, kind="ExternalInput").ap()
    hb["eye_bf"] = nc.dram_tensor("eye_bf", [128, 128], BF16, kind="ExternalInput").ap()
    hb["out"] = nc.dram_tensor("out", [NSEQ * SEQ, D], F32, kind="ExternalOutput").ap()

    es = contextlib.ExitStack()
    E = es.enter_context
    sb = {}

    def S_(name, shape, dtype):
        sb[name] = E(nc.sbuf_tensor(name, shape, dtype))

    S_("xa", [128, 2 * NG * D], F32)
    S_("tb", [128, 2 * TB], F32)
    S_("kh_f", [128, NKI * TB], BF16)
    S_("kwd_f", [128, NKI * D], BF16)
    S_("wstage", [128, I], F32)
    S_("kq_d", [128, I], BF16)
    S_("wq_scr", [128, I], F32)
    S_("iwc", [128, I], mybir.dt.int32)
    S_("iq2", [128, TB], mybir.dt.int32)
    S_("deq", [128, I], BF16)
    S_("aw", [128, 1], F32)
    S_("wrec", [128, 1], F32)
    S_("swd", [128, NKD], F32)
    S_("alpha_u", [128, NKI], F32)
    S_("c2", [128, NKI], F32)
    S_("m2row", [1, 8], F32)
    S_("clipd", [1, 1], F32)
    S_("sigd", [1, 1], F32)
    S_("shv", [1, 1], F32)
    S_("shiv", [1, 1], F32)
    S_("sh_bc", [128, 1], F32)
    S_("shi_bc", [128, 1], F32)
    S_("v_chunk", [128, TB], F32)
    S_("d_scr2", [128, D], F32)
    S_("out_s", [128, NG * D], F32)
    S_("eye_bf_s", [128, 128], BF16)

    ps = {}
    ps["pT0"] = E(nc.psum_tensor("pT0", [128, 1024], BF16))
    ps["pT1"] = E(nc.psum_tensor("pT1", [128, 1024], BF16))
    ps["pd"] = E(nc.psum_tensor("pd", [128, D], F32))

    sems = {}
    for s in ["s_x", "s_tl", "s_tv2", "s_kh", "s_dmm", "s_ofree", "s_wld", "s_wq",
              "s_wt", "s_wev", "s_vs", "s_small", "s_sig", "s_bc1", "s_dbb2",
              "s_pre", "s_st"]:
        sems[s] = E(nc.semaphore(s))
    lo = min(h.num for h in sems.values())
    hi = max(h.num for h in sems.values())
    g = sems
    pT = [ps["pT0"], ps["pT1"]]

    with nc.Block() as blk:
        @blk.sync
        def _(sync):
            for name, src in [("alpha_u", hb["alpha_u_i"]), ("m2row", hb["maxes2"]),
                              ("clipd", hb["clip_down"]), ("eye_bf_s", hb["eye_bf"])]:
                sync.dma_start(out=sb[name][:], in_=src).then_inc(g["s_small"], 16)
            for wi in range(NKD):
                if wi >= 1:
                    sync.wait_ge(g["s_wld"], 16 * wi)
                    sync.wait_ge(g["s_wq"], wi)
                sync.dma_start(out=sb["wstage"][:],
                               in_=hb["w_down"][wi * 128:(wi + 1) * 128, :]).then_inc(g["s_wld"], 16)
            sync.wait_ge(g["s_bc1"], 1)
            _bcast_col(sync, sb["sh_bc"], 1, g["s_dbb2"], 0)
            _bcast_col(sync, sb["shi_bc"], 1, g["s_dbb2"], 16 * 7)
            sync.nop().then_inc(g["s_dbb2"], 1)
            for j in range(NBLK):
                if j >= 1:
                    sync.wait_ge(g["s_x"], 16 * j)
                if j >= 2:
                    sync.wait_ge(g["s_ofree"], 4 * (j - 1))
                sync.dma_start(out=_g3(sb["xa"][:, (j % 2) * NG * D:(j % 2 + 1) * NG * D]),
                               in_=_tm(hb["xmid"][j * TB:(j + 1) * TB, :])).then_inc(g["s_x"], 16)
                if j >= 1:
                    sync.wait_ge(g["s_ofree"], 4 * j)
                    if j >= 2:
                        sync.wait_ge(g["s_st"], 16 * (j - 1))
                    sync.dma_start(out=_tm(hb["out"][(j - 1) * TB:j * TB, :]),
                                   in_=_g3(sb["out_s"][:])).then_inc(g["s_st"], 16)
            sync.wait_ge(g["s_ofree"], 4 * NBLK)
            sync.wait_ge(g["s_st"], 16 * (NBLK - 1))
            sync.dma_start(out=_tm(hb["out"][(NBLK - 1) * TB:NBLK * TB, :]),
                           in_=_g3(sb["out_s"][:])).then_inc(g["s_st"], 16)
            sync.wait_ge(g["s_st"], 16 * NBLK)

        @blk.gpsimd
        def _(gpsimd):
            for j in range(NBLK):
                for m in range(NKI):
                    mh = j * NKI + m
                    if mh >= 1:
                        gpsimd.wait_ge(g["s_tl"], 16 * mh)
                    if mh >= 2:
                        gpsimd.wait_ge(g["s_tv2"], mh - 1)
                    gpsimd.dma_start(out=sb["tb"][:, (mh % 2) * TB:(mh % 2 + 1) * TB],
                                     in_=hb["t_in"][mh * 128:(mh + 1) * 128, :]).then_inc(g["s_tl"], 16)

        @blk.scalar
        def _(scalar):
            scalar.wait_ge(g["s_small"], 16 * 4)
            scalar.activation(sb["sigd"][:], sb["clipd"][:], AF.Sigmoid).then_inc(g["s_sig"], 1)
            for wi in range(NKD):
                for m in range(NKI):
                    scalar.wait_ge(g["s_wt"], wi * NKI + m + 1)
                    scalar.copy(sb["kwd_f"][:, m * D + wi * 128:m * D + (wi + 1) * 128],
                                pT[m % 2][:, 0:128]).then_inc(g["s_wev"], 1)

        @blk.tensor
        def _(tensor):
            tensor.wait_ge(g["s_small"], 16 * 4)
            for wi in range(NKD):
                tensor.wait_ge(g["s_wq"], wi + 1)
                for m in range(NKI):
                    if wi * NKI + m >= 2:
                        tensor.wait_ge(g["s_wev"], wi * NKI + m - 1)
                    tensor.transpose(pT[m % 2][:, 0:128],
                                     sb["deq"][:, m * 128:(m + 1) * 128],
                                     sb["eye_bf_s"][:]).then_inc(g["s_wt"], 1)
            tensor.wait_ge(g["s_wev"], NKD * NKI)
            tensor.wait_ge(g["s_pre"], 1)
            for j in range(NBLK):
                tensor.wait_ge(g["s_kh"], j + 1)
                for gg in range(NG):
                    if 4 * j + gg >= 1:
                        tensor.wait_ge(g["s_ofree"], 4 * j + gg)
                    for m in range(NKI):
                        for hf in range(2):
                            ins = tensor.matmul(ps["pd"][:, hf * TB:(hf + 1) * TB],
                                                sb["kh_f"][:, m * TB + gg * 128:m * TB + (gg + 1) * 128],
                                                sb["kwd_f"][:, m * D + hf * TB:m * D + (hf + 1) * TB],
                                                start=(m == 0), stop=(m == NKI - 1))
                            if m == NKI - 1 and hf == 1:
                                ins.then_inc(g["s_dmm"], 1)

        @blk.vector
        def _(vector):
            vsn = [0]

            def vq(ins):
                vsn[0] += 1
                ins.then_inc(g["s_vs"], 1)
                vector.wait_ge(g["s_vs"], vsn[0])

            for wi in range(NKD):
                vector.wait_ge(g["s_wld"], 16 * (wi + 1))
                if wi >= 1:
                    vector.wait_ge(g["s_wt"], wi * NKI)
                vq(vector.tensor_reduce(sb["aw"][:], sb["wstage"][:], AX, OP.max,
                                        apply_absolute_value=True))
                vq(vector.tensor_scalar(sb["swd"][:, wi:wi + 1], sb["aw"][:], 1.0 / 127.0, None, OP.mult))
                vq(vector.reciprocal(sb["wrec"][:], sb["swd"][:, wi:wi + 1]))
                vq(vector.tensor_scalar(sb["wq_scr"][:], sb["wstage"][:], sb["wrec"][:], None, OP.mult))
                vq(vector.tensor_copy(sb["iwc"][:], sb["wq_scr"][:]))
                vector.tensor_scalar(sb["deq"][:], sb["iwc"][:], sb["swd"][:, wi:wi + 1],
                                     None, OP.mult).then_inc(g["s_wq"], 1)
                vector.wait_ge(g["s_wq"], wi + 1)
            vector.wait_ge(g["s_sig"], 1)
            vq(vector.tensor_reduce(sb["shv"][:], sb["m2row"][:], AX, OP.max))
            vq(vector.tensor_tensor(sb["shv"][:], sb["shv"][:], sb["sigd"][:], OP.mult))
            vq(vector.tensor_scalar(sb["shv"][:], sb["shv"][:], 1.0 / 127.0, None, OP.mult))
            vq(vector.reciprocal(sb["shiv"][:], sb["shv"][:]))
            vq(vector.tensor_copy(sb["sh_bc"][0:1, :], sb["shv"][:]))
            vector.tensor_copy(sb["shi_bc"][0:1, :], sb["shiv"][:]).then_inc(g["s_bc1"], 1)
            vector.wait_ge(g["s_dbb2"], 16 * 14 + 1)
            vq(vector.tensor_scalar(sb["c2"][:], sb["alpha_u"][:], sb["shi_bc"][:], None, OP.mult))
            vector.nop().then_inc(g["s_pre"], 1)
            for j in range(NBLK):
                o = (j % 2) * NG * D
                if j >= 1:
                    vector.wait_ge(g["s_dmm"], 4 * j)
                    vector.wait_ge(g["s_st"], 16 * j)
                for m in range(NKI):
                    mh = j * NKI + m
                    vector.wait_ge(g["s_tl"], 16 * (mh + 1))
                    vector.tensor_scalar(sb["v_chunk"][:], sb["tb"][:, (mh % 2) * TB:(mh % 2 + 1) * TB],
                                         sb["c2"][:, m:m + 1], None, OP.mult).then_inc(g["s_tv2"], 1)
                    vector.wait_ge(g["s_tv2"], mh + 1)
                    vq(vector.tensor_scalar(sb["v_chunk"][:], sb["v_chunk"][:],
                                            -128.0, 127.0, OP.max, OP.min))
                    vq(vector.tensor_copy(sb["iq2"][:], sb["v_chunk"][:]))
                    ins = vector.tensor_copy(sb["kh_f"][:, m * TB:(m + 1) * TB], sb["iq2"][:])
                    if m == NKI - 1:
                        ins.then_inc(g["s_kh"], 1)
                    else:
                        vq(ins)
                vector.wait_ge(g["s_x"], 16 * (j + 1))
                for gg in range(NG):
                    vector.wait_ge(g["s_dmm"], 4 * j + gg + 1)
                    vq(vector.tensor_scalar(sb["d_scr2"][:], ps["pd"][:], sb["sh_bc"][:], None, OP.mult))
                    vector.tensor_tensor(sb["out_s"][:, gg * D:(gg + 1) * D],
                                         sb["d_scr2"][:], sb["xa"][:, o + gg * D:o + (gg + 1) * D],
                                         OP.add).then_inc(g["s_ofree"], 1)
                    vector.wait_ge(g["s_ofree"], 4 * j + gg + 1)

    nc.gpsimd.dma_reset(range(lo, hi + 1))
    nc.gpsimd.sem_clear(range(lo, hi + 1))
    es.close()
    return nc


def _stage_phase_b(inputs, core, res_a):
    return {
        "xmid": res_a[core]["xmid"],
        "m2_all": res_a[core]["m2_all"],
        "r2_all": res_a[core]["r2_all"],
        "maxes1": np.stack([res_a[c]["max1"][0] for c in range(8)], axis=1).reshape(1, 8),
        "clip_x": inputs["L0_ffn_clip_x"].reshape(1, 1),
        "w_gate": np.ascontiguousarray(inputs["L0_ffn_w_gate"]),
        "w_up": np.ascontiguousarray(inputs["L0_ffn_w_up"]),
        "eye_bf": np.eye(128, dtype=ml_dtypes.bfloat16),
    }


def _stage_phase_c(inputs, core, res_a, res_b):
    return {
        "xmid": res_a[core]["xmid"],
        "t_in": res_b[core]["t_out"],
        "alpha_u_i": res_b[core]["alpha_u_o"],
        "maxes2": np.stack([res_b[c]["max2"][0] for c in range(8)], axis=1).reshape(1, 8),
        "clip_down": inputs["L0_ffn_clip_down"].reshape(1, 1),
        "w_down": np.ascontiguousarray(inputs["L0_ffn_w_down"]),
        "eye_bf": np.eye(128, dtype=ml_dtypes.bfloat16),
    }


def kernel(**inputs):
    nc_a = build_phase_a()
    res_a = run_bass_kernel_spmd(nc_a, [_stage_phase_a(inputs, c) for c in range(8)],
                                 core_ids=list(range(8))).results
    nc_b = build_phase_b()
    res_b = run_bass_kernel_spmd(nc_b, [_stage_phase_b(inputs, c, res_a) for c in range(8)],
                                 core_ids=list(range(8))).results
    nc_c = build_phase_c()
    res_c = run_bass_kernel_spmd(nc_c, [_stage_phase_c(inputs, c, res_a, res_b) for c in range(8)],
                                 core_ids=list(range(8))).results
    out = np.stack([res_c[c]["out"].reshape(NSEQ, SEQ, D) for c in range(8)])
    return out.reshape(16, SEQ, D)
